# version 23
# baseline (speedup 1.0000x reference)
import numpy as np
from contextlib import ExitStack
import concourse.bass as bass
import concourse.mybir as mybir
from concourse.bass_utils import run_bass_kernel_spmd

F32 = mybir.dt.float32
BF16 = mybir.dt.bfloat16
I32 = mybir.dt.int32
AF = mybir.ActivationFunctionType
ALU = mybir.AluOpType

NCORES = 8
T = 2048
D = 2048
DC = 16
HID = 5632
HC = 44
NH = 16
HD = 128
BLK = 256
MT = 512
NMT = T // MT
TC = MT // 128
NB = MT // BLK
EPS_LN = 1e-5
EPS_RMS = 1e-6


class Sem:
    def __init__(self, nc, es, name):
        self.h = es.enter_context(nc.semaphore(name))
        self.n = 0
        self.name = name

    def bump(self, k):
        self.n += k
        return self.n


class Em:
    ENGS = ("sync", "gpsimd", "vector", "scalar", "tensor")

    def __init__(self):
        self.q = {e: [] for e in self.ENGS}
        self.autosync = {}
        self._last_auto = {}
        self.mm_labels = []
        self.phase = "setup"

    def op(self, eng, meth, *a, waits=(), inc=None, incn=1, **kw):
        if eng == "tensor" and meth in ("matmul", "transpose"):
            self.mm_labels.append(self.phase)
        if eng in self.autosync:
            s0 = self.autosync[eng]
            if s0.n > 0 and self._last_auto.get(eng, -1) < s0.n:
                self.q[eng].append(("wait", s0, s0.n))
                self._last_auto[eng] = s0.n
        ws = [(s, v) for (s, v) in waits if v > 0]
        for (s, v) in ws:
            assert v <= s.n, f"wait on future: {s.name} {v} > {s.n} ({eng}.{meth})"
        inc.bump(incn) if inc is not None else None
        self.q[eng].append(("op", meth, a, kw, ws, inc, incn))

    def wait(self, eng, sem, val):
        if val <= 0:
            return
        assert val <= sem.n, f"wait on future: {sem.name} {val} > {sem.n}"
        self.q[eng].append(("wait", sem, val))

    def replay(self, eng_name, e):
        for item in self.q[eng_name]:
            if item[0] == "wait":
                _, sem, val = item
                e.wait_ge(sem.h, val)
            else:
                _, meth, a, kw, ws, inc, incn = item
                for (s, v) in ws:
                    e.wait_ge(s.h, v)
                instr = getattr(e, meth)(*a, **kw)
                if inc is not None:
                    instr.then_inc(inc.h, incn)


def build_program():
    nc = bass.Bass("TRN2", target_bir_lowering=False)
    es = ExitStack()

    xin = nc.dram_tensor("xin", [D, T], F32, kind="ExternalInput").ap()
    wg = nc.dram_tensor("wg", [HC, 128, DC, 128], I32, kind="ExternalInput").ap()
    wu = nc.dram_tensor("wu", [HC, 128, DC, 128], I32, kind="ExternalInput").ap()
    wd = nc.dram_tensor("wd", [DC, 2, 128, 22, 128], I32, kind="ExternalInput").ap()
    wqkv = nc.dram_tensor("wqkv", [48, 128, DC, 128], F32, kind="ExternalInput").ap()
    wvin = nc.dram_tensor("wvin", [8, 128, DC, 256], F32, kind="ExternalInput").ap()
    wga = nc.dram_tensor("wga", [DC, 128, DC, 128], F32, kind="ExternalInput").ap()
    wo = nc.dram_tensor("wo", [DC, 128, DC, 128], F32, kind="ExternalInput").ap()
    sgate = nc.dram_tensor("sgate", [128, HC], F32, kind="ExternalInput").ap()
    sup = nc.dram_tensor("sup", [128, HC], F32, kind="ExternalInput").ap()
    sdown = nc.dram_tensor("sdown", [128, DC], F32, kind="ExternalInput").ap()
    nwin = nc.dram_tensor("nwin", [128, DC], F32, kind="ExternalInput").ap()
    slopes = nc.dram_tensor("slopes", [1, NH], F32, kind="ExternalInput").ap()
    ccm = nc.dram_tensor("ccm", [1, 4], F32, kind="ExternalInput").ap()
    ccp = nc.dram_tensor("ccp", [1, 4], F32, kind="ExternalInput").ap()
    out = nc.dram_tensor("out", [D, T], F32, kind="ExternalOutput").ap()

    x1h = nc.dram_tensor("x1h", [D, T], F32, kind="Internal").ap()
    x2h = nc.dram_tensor("x2h", [D, T], F32, kind="Internal").ap()
    oh = nc.dram_tensor("oh", [D, T], F32, kind="Internal").ap()
    qgh = nc.dram_tensor("qgh", [D, T], BF16, kind="Internal").ap()
    gah = nc.dram_tensor("gah", [D, T], BF16, kind="Internal").ap()
    wgb = nc.dram_tensor("wgb", [HC, 128, DC, 128], BF16, kind="Internal").ap()
    wub = nc.dram_tensor("wub", [HC, 128, DC, 128], BF16, kind="Internal").ap()
    wdb = nc.dram_tensor("wdb", [DC, 2, 128, 22, 128], BF16, kind="Internal").ap()
    wqkvb = nc.dram_tensor("wqkvb", [48, 128, DC, 128], BF16, kind="Internal").ap()
    wvb = nc.dram_tensor("wvb", [8, 128, DC, 256], BF16, kind="Internal").ap()
    wgab = nc.dram_tensor("wgab", [DC, 128, DC, 128], BF16, kind="Internal").ap()
    wob = nc.dram_tensor("wob", [DC, 128, DC, 128], BF16, kind="Internal").ap()
    cci = nc.dram_tensor("cci", [NH * HD, HD], F32, kind="Internal").ap()
    ccs_d = nc.dram_tensor("ccs_d", [1, 4 * NH], F32, kind="Internal").ap()
    cco = nc.dram_tensor("cco", [4 * NH * HD, HD], F32, kind="Internal").ap()

    def sb(name, shape, dt):
        return es.enter_context(nc.sbuf_tensor(name, shape, dt))

    AR_H = sb("AR_H", [128, 32768], BF16)
    AR_X = sb("AR_X", [128, 8192], BF16)
    AR_L = sb("AR_L", [128, 8192], BF16)
    AR_W = sb("AR_W", [128, 23552], BF16)

    L_sb = sb("L_sb", [128, NH, HD], F32)
    Lbf = sb("Lbf", [128, 2, HD], BF16)
    s128n = sb("s128n", [128, MT], F32)
    shn = sb("shn", [128, MT], F32)
    mbase = sb("mbase", [128, BLK], BF16)
    diffc = sb("diffc", [128, 2, BLK], BF16)
    ddT = sb("ddT", [128, 2, 2, BLK], BF16)
    qdec = sb("qdec", [128, 2, BLK], BF16)
    kdecT = sb("kdecT", [128, NH, 2], F32)
    nvals = sb("nvals", [128, 2], F32)
    Db = sb("Db", [128, 8, NH], F32)
    srep = sb("srep", [128, NH], F32)
    snegrep = sb("snegrep", [128, NH], F32)
    wrep = sb("wrep", [128, 4, NH], F32)
    slop = sb("slop", [1, NH], F32)
    ccw = sb("ccw", [1, 4, NH], F32)
    ccm_sb = sb("ccm_sb", [1, 4], F32)
    ccp_sb = sb("ccp_sb", [1, 4], F32)
    ones_bf = sb("ones_bf", [128, 128], BF16)
    ident = sb("ident", [128, 128], BF16)
    sg_sb = sb("sg_sb", [128, HC], F32)
    su_sb = sb("su_sb", [128, HC], F32)
    sdh_sb = sb("sdh_sb", [128, DC], F32)
    nw_sb = sb("nw_sb", [128, DC], F32)
    Mt = sb("Mt", [128, 512], F32)
    Rt = sb("Rt", [128, 512], F32)
    tg_s = sb("tg_s", [128, 512], F32)
    t1_s = sb("t1_s", [128, 512], F32)
    XBG = sb("XBG", [128, DC, MT], BF16)
    SQ = sb("SQ", [128, 2, MT], BF16)
    sg_scr = sb("sg_scr", [128, 2, 512], BF16)
    qs_s = sb("qs_s", [128, 2, BLK], BF16)
    qg_s = sb("qg_s", [128, 2, 512], BF16)
    qkm_s = sb("qkm_s", [128, 2, 2, BLK], BF16)
    kT_s = sb("kT_s", [128, 2, 2, 128], BF16)
    o_s = sb("o_s", [128, 2, 512], F32)
    ga_s = sb("ga_s", [128, 2, 512], BF16)
    xd_s = sb("xd_s", [128, 2, 512], F32)
    qgl_s = sb("qgl_s", [128, 2, 512], BF16)

    M2 = tg_s
    R2 = Rt
    tu_s = t1_s
    Ssbf = AR_H[:, 28672:30720].rearrange("p (h e) -> p h e", h=NH)

    def psum(name, shape, dt):
        return es.enter_context(nc.psum_tensor(name, shape, dt))

    PS = [psum(f"PS{i}", [128, 512], F32) for i in range(7)]
    PTR = psum("PTR", [128, 512], BF16)
    PS6B = PS[6][:].bitcast(BF16)

    def S(name):
        return Sem(nc, es, name)

    pe, dv, ac, gp = S("pe"), S("dv"), S("ac"), S("gp")
    sx = [S("sx0"), S("sx1")]
    sxd = [S("sxd0"), S("sxd1")]
    sqg = [S("sqg0"), S("sqg1")]
    sol, sgl = S("sol"), S("sgl")
    swA = [S("swA0"), S("swA1"), S("swA2")]
    swB = [S("swB0"), S("swB1"), S("swB2")]
    swC = [S("swC0"), S("swC1")]
    sfA = [S("sfA0"), S("sfA1"), S("sfA2")]
    sfB = [S("sfB0"), S("sfB1"), S("sfB2")]
    sfC = [S("sfC0"), S("sfC1")]
    sbg = S("sbg")
    sto = [S("sto0"), S("sto1")]
    stq = [S("stq0"), S("stq1")]
    stg = [S("stg0"), S("stg1")]
    std = [S("std0"), S("std1")]
    scc = S("scc")
    scc2 = S("scc2")
    srw = S("srw")
    swb = S("swb")
    ccd = S("ccd")
    scst = S("scst")

    em = Em()
    em.autosync = {"vector": dv, "scalar": ac}

    H_f32 = AR_H[:].bitcast(F32)
    AXf = AR_X[:].bitcast(F32)

    def Hchunk(h):
        return AR_H[:, h * MT:(h + 1) * MT]

    def qv(h):
        return AR_H[:, h * MT:(h + 1) * MT]

    def kv_(h):
        return AR_H[:, 8192 + h * MT:8192 + (h + 1) * MT]

    def ofull(h):
        return H_f32[:, h * MT:(h + 1) * MT]

    def vT(tc):
        return AR_X[:, tc * 2048:(tc + 1) * 2048]

    def lnc(k):
        return AR_L[:, k * MT:(k + 1) * MT]

    def gal(h):
        return AR_L[:, h * MT:(h + 1) * MT]

    def wgs(s):
        return AR_W[:, s * 2048:(s + 1) * 2048].rearrange("p (c o) -> p c o", c=DC)

    def wus(s):
        return AR_W[:, 6144 + s * 2048:6144 + (s + 1) * 2048].rearrange(
            "p (c o) -> p c o", c=DC)

    def wvs(s):
        return AR_W[:, 6144 + s * 4096:6144 + (s + 1) * 4096].rearrange(
            "p (c o) -> p c o", c=DC)

    def wds(s):
        return AR_W[:, 12288 + s * 5632:12288 + (s + 1) * 5632].rearrange(
            "p (a c o) -> p a c o", a=2, c=22)

    psum_free = [0] * 8
    psum_free_sem = [dv] * 8
    sq_war = [0, 0]
    xbg_rd = {"pe": 0, "dv": 0}
    war = {"a_sg": 0, "a_m2": 0, "a_r": 0}
    freeH = []
    freeL = []
    freeX = []
    bg_jobs = []
    bg_total = [0]
    wbmarks = {}

    def bg_emit(k):
        while bg_jobs and k > 0:
            dst_ap, src_ap = bg_jobs.pop(0)
            em.op("gpsimd", "dma_start", out=dst_ap, in_=src_ap,
                  inc=sbg, incn=16)
            k -= 1
        bg_total[0] = sbg.n

    def psfree(i):
        return [(psum_free_sem[i], psum_free[i])] if psum_free[i] else []

    def set_psfree(i, sem, val):
        psum_free_sem[i] = sem
        psum_free[i] = val

    def setup():
        em.op("sync", "dma_start", out=sg_sb[:], in_=sgate, inc=scst, incn=16)
        em.op("sync", "dma_start", out=su_sb[:], in_=sup, inc=scst, incn=16)
        em.op("sync", "dma_start", out=sdh_sb[:], in_=sdown, inc=scst, incn=16)
        em.op("sync", "dma_start", out=nw_sb[:], in_=nwin, inc=scst, incn=16)
        em.op("sync", "dma_start", out=slop[:], in_=slopes, inc=scst, incn=16)
        em.op("sync", "dma_start", out=ccm_sb[:], in_=ccm, inc=scst, incn=16)
        em.op("sync", "dma_start", out=ccp_sb[:], in_=ccp, inc=scst, incn=16)
        ldall = (scst, scst.n)

        em.op("vector", "memset", ones_bf[:], 1.0, inc=dv)
        em.op("vector", "memset", L_sb[:].rearrange("p h e -> p (h e)"), 0.0,
              inc=dv)
        v_ones = dv.n
        em.op("vector", "tensor_scalar_mul", sdh_sb[:], sdh_sb[:], 0.5,
              waits=[ldall], inc=dv)
        em.op("gpsimd", "affine_select", ident[:], ones_bf[:],
              pattern=[[1, 128]], compare_op=ALU.is_equal, fill=0.0, base=0,
              channel_multiplier=-1, waits=[(dv, v_ones)], inc=gp)
        em.op("gpsimd", "iota", mbase[:], pattern=[[1, BLK]], base=1,
              channel_multiplier=0, allow_small_or_imprecise_dtypes=True,
              inc=gp)
        em.op("gpsimd", "iota", diffc[:], pattern=[[-128, 2], [1, BLK]],
              base=0, channel_multiplier=-1,
              allow_small_or_imprecise_dtypes=True, inc=gp)
        em.op("gpsimd", "iota", nvals[:], pattern=[[128, 2]], base=-255,
              channel_multiplier=1, allow_small_or_imprecise_dtypes=True,
              inc=gp)
        g_iota = gp.n
        em.op("vector", "tensor_single_scalar", ddT[:, 0], diffc[:], 0.0,
              ALU.is_lt, waits=[(gp, g_iota)], inc=dv)
        em.op("vector", "tensor_scalar_max", diffc[:], diffc[:], 0.0, inc=dv)
        em.op("vector", "scalar_tensor_tensor", diffc[:], ddT[:, 0], 1.0e6,
              diffc[:], ALU.mult, ALU.add, inc=dv)
        em.op("sync", "dma_start", out=srep[:],
              in_=slopes.broadcast_to([128, NH]), waits=[ldall],
              inc=scst, incn=16)
        g_rep = 0
        rep_ld = (scst, scst.n)
        em.op("vector", "tensor_scalar_mul", snegrep[:], srep[:], -1.0,
              waits=[rep_ld], inc=dv)
        v_srep = dv.n
        for b in range(8):
            em.op("scalar", "activation", Db[:, b, :], srep[:], AF.Exp,
                  scale=float(-256.0 * b),
                  waits=[rep_ld, (dv, v_srep)], inc=ac)
        for h in range(NH):
            em.op("scalar", "activation", kdecT[:, h, :], nvals[:], AF.Exp,
                  scale=srep[:, h:h + 1],
                  waits=[(dv, v_srep), (gp, g_iota), rep_ld], inc=ac)
        for j in range(4):
            em.op("scalar", "activation", ccw[:, j, :], slop[:], AF.Exp,
                  scale=ccp_sb[:, j:j + 1], waits=[ldall], inc=ac)
        a_ccw = ac.n
        for j in range(4):
            em.op("vector", "tensor_scalar_mul", ccw[:, j, :], ccw[:, j, :],
                  ccm_sb[:, j:j + 1], waits=[(ac, a_ccw)], inc=dv)
        v_ccw = dv.n
        em.op("gpsimd", "dma_start", out=ccs_d,
              in_=ccw[:].rearrange("p j h -> p (j h)"),
              waits=[(dv, v_ccw)], inc=srw, incn=16)
        em.op("gpsimd", "dma_start",
              out=wrep[:].rearrange("p j h -> p (j h)"),
              in_=ccs_d.broadcast_to([128, 4 * NH]),
              waits=[(srw, srw.n)], inc=srw, incn=16)
        setup.marks = {"dv": dv.n, "ac": ac.n, "gp": gp.n,
                       "wrep": (srw, srw.n)}

    def stats_chunk(k, xsrc_ap, ld_wait, sq_extra=()):
        sl = k % 2
        wq = list(ld_wait) + list(sq_extra)
        if sq_war[sl]:
            wq.append((pe, sq_war[sl]))
        em.op("scalar", "activation", SQ[:, sl, :], xsrc_ap, AF.Square,
              waits=wq, inc=ac)
        va = ac.n
        wts = list(ld_wait) + (psfree(4) if k == 0 else [])
        em.op("tensor", "matmul", PS[4][:], ones_bf[:], xsrc_ap,
              start=(k == 0), stop=(k == DC - 1), waits=wts, inc=pe)
        wts = [(ac, va)] + (psfree(5) if k == 0 else [])
        em.op("tensor", "matmul", PS[5][:], ones_bf[:], SQ[:, sl, :],
              start=(k == 0), stop=(k == DC - 1), waits=wts, inc=pe)
        sq_war[sl] = pe.n

    def layernorm(mt, src, want_s, pre=None):
        em.phase = f"LN(pre={pre is not None})"
        """LN over features, input resident in XBG (bf16). If pre is None,
        fill XBG from HBM src and accumulate stats; else pre is the pe-mark
        of already-accumulated stats in PS4 (sum) / PS5 (sumsq)."""
        t0 = mt * MT
        if pre is None:
            for k in range(DC):
                sl = k % 2
                wxd = [(std[sl], std[sl].n)]
                if xd_free[sl][0]:
                    wxd += [(dv, xd_free[sl][0]), (ac, xd_free[sl][1])]
                em.op("sync", "dma_start", out=xd_s[:, sl, :],
                      in_=src[k * 128:(k + 1) * 128, t0:t0 + MT],
                      waits=wxd, inc=sxd[sl], incn=16)
                x_ld = sxd[sl].n
                w0 = [(sxd[sl], x_ld)]
                if k == 0:
                    w0.append((pe, xbg_rd["pe"]))
                em.op("vector", "tensor_copy", XBG[:, k, :], xd_s[:, sl, :],
                      waits=w0, inc=dv)
                vb = dv.n
                wq = [(sxd[sl], x_ld)]
                if sq_war[sl]:
                    wq.append((pe, sq_war[sl]))
                em.op("scalar", "activation", SQ[:, sl, :], xd_s[:, sl, :],
                      AF.Square, waits=wq, inc=ac)
                va = ac.n
                wts = [(dv, vb)] + (psfree(4) if k == 0 else [])
                em.op("tensor", "matmul", PS[4][:], ones_bf[:], XBG[:, k, :],
                      start=(k == 0), stop=(k == DC - 1), waits=wts, inc=pe)
                wts = [(ac, va)] + (psfree(5) if k == 0 else [])
                em.op("tensor", "matmul", PS[5][:], ones_bf[:], SQ[:, sl, :],
                      start=(k == 0), stop=(k == DC - 1), waits=wts, inc=pe)
                sq_war[sl] = pe.n
                xd_free[sl] = (vb, va)
            p_stats = pe.n
        else:
            p_stats = pre
        em.op("vector", "tensor_scalar_mul", Mt[:], PS[4][:], 1.0 / D,
              waits=[(pe, p_stats), (ac, war["a_m2"])], inc=dv)
        set_psfree(4, dv, dv.n)
        v_m = dv.n
        em.op("scalar", "activation", M2[:], Mt[:], AF.Square,
              waits=[(dv, v_m)], inc=ac)
        a_m2 = ac.n
        war["a_m2"] = a_m2
        em.op("vector", "scalar_tensor_tensor", Rt[:], PS[5][:], 1.0 / D,
              M2[:], ALU.mult, ALU.subtract,
              waits=[(ac, a_m2), (ac, war["a_r"])], inc=dv)
        set_psfree(5, dv, dv.n)
        em.op("vector", "tensor_scalar_add", Rt[:], Rt[:], EPS_LN, inc=dv)
        em.op("vector", "reciprocal", Rt[:], Rt[:], inc=dv)
        v_v = dv.n
        em.op("scalar", "activation", Rt[:], Rt[:], AF.Sqrt,
              waits=[(dv, v_v)], inc=ac)
        a_r = ac.n
        war["a_r"] = a_r
        first_write = True
        nmarks = []
        ps_started = [False]
        for cg in range(4):
            Mbc = Mt[:].unsqueeze(1).broadcast_to([128, 4, 512])
            Rbc = Rt[:].unsqueeze(1).broadcast_to([128, 4, 512])
            dstv = AR_L[:, cg * 2048:(cg + 1) * 2048].rearrange(
                "p (c t) -> p c t", c=4)
            wl = [(pe, p_stats)] if cg == 0 else []
            if first_write:
                wl += list(freeL)
                freeL.clear()
                first_write = False
            em.op("vector", "tensor_tensor", dstv,
                  XBG[:, cg * 4:(cg + 1) * 4, :], Mbc, ALU.subtract,
                  waits=wl, inc=dv)
            em.op("vector", "tensor_tensor", dstv, dstv, Rbc, ALU.mult,
                  waits=([(ac, a_r)] if cg == 0 else []), inc=dv)
            nmarks.append(dv.n)
            if want_s:
                for j in range(4):
                    k = cg * 4 + j
                    wts = [(dv, nmarks[-1])] if j == 0 else []
                    if k == 0:
                        wts = wts + psfree(4)
                    em.op("tensor", "matmul", PS[4][:], ones_bf[:], lnc(k),
                          start=(k == 0), stop=(k == DC - 1), waits=wts,
                          inc=pe)
        s_mark = 0
        if want_s:
            p_s = pe.n
            em.op("vector", "tensor_scalar_mul", s128n[:], PS[4][:], -128.0,
                  waits=[(pe, p_s)], inc=dv)
            set_psfree(4, dv, dv.n)
            s_mark = dv.n
        xbg_rd["pe"] = p_stats
        xbg_rd["dv"] = nmarks[-1]
        return nmarks, s_mark

    class WStream:
        def __init__(self, sems_cast, sems_bf, slot_fn, init_free, nslots=2,
                     gate=None):
            self.sc, self.sf = sems_cast, sems_bf
            self.slot = slot_fn
            self.ns = nslots
            self.freeat = [init_free] * nslots
            self.loaded = [None] * nslots
            self.wb = [(swb, swb.n) if swb.n else None] * nslots
            self.gate = gate

        def issue(self, i, src_ap, cached, dst=None, bg=0, wb_dst=None):
            s = i % self.ns
            w = [self.freeat[s]] if self.freeat[s] is not None else []
            if self.wb[s] is not None:
                w.append(self.wb[s])
                self.wb[s] = None
            tgt = dst if dst is not None else self.slot(s)[:]
            if cached:
                w.append(self.gate if self.gate else (sbg, bg_total[0]))
                em.op("scalar", "dma_start", out=tgt, in_=src_ap,
                      waits=w, inc=self.sf[s], incn=16)
                self.loaded[s] = (self.sf[s], self.sf[s].n)
            else:
                em.op("gpsimd", "dma_start", out=tgt, in_=src_ap,
                      waits=w, inc=self.sc[s], incn=16)
                self.loaded[s] = (self.sc[s], self.sc[s].n)
                if wb_dst is not None:
                    em.op("sync", "dma_start", out=wb_dst, in_=tgt,
                          waits=[self.loaded[s]], inc=swb, incn=16)
                    self.wb[s] = (swb, swb.n)
            if bg:
                bg_emit(bg)

        def consumed(self, i, mark):
            self.freeat[i % self.ns] = mark

    def ffn(mt, call, dst):
        t0 = mt * MT
        sgH_mark = ffn.sgH

        cached = (call == 1) or mt >= 2
        pmark = (pe, pe.n)
        stG = WStream(swA, sfA, wgs, pmark, nslots=3)
        stU = WStream(swB, sfB, wus, pmark, nslots=3)
        stD = WStream(swC, sfC, wds, pmark)

        def gsrc(h):
            return (wgb if cached else wg)[h].rearrange("p c o -> p c o")

        def usrc(h):
            return (wub if cached else wu)[h].rearrange("p c o -> p c o")

        def dsrc(d):
            return (wdb if cached else wd)[d].rearrange(
                "d2 p c o -> p d2 c o")

        for i0 in range(3):
            stG.issue(i0, gsrc(i0), cached)
            stU.issue(i0, usrc(i0), cached)
        nmarks, s_mark = layernorm(mt, xin if call == 0 else x2h, True,
                                   stats_pre[0])
        stats_pre[0] = None
        ln_done = nmarks[3]
        em.phase = f"ffn{call}-gateup"
        h_last = 0
        first_H = True
        sh_q = []
        for h in range(HC):
            s = h % 3
            gi, ui = h % 2, 2 + h % 2
            wts = [stG.loaded[s]] + psfree(gi)
            if h > 0:
                wts.append((dv, ln_done))
            for k in range(DC):
                kw = wts if k == 0 else []
                if h == 0:
                    kw = list(kw) + [(dv, nmarks[k // 4])]
                em.op("tensor", "matmul", PS[gi][:], wgs(s)[:, k, :], lnc(k),
                      start=(k == 0), stop=(k == DC - 1),
                      waits=kw,
                      inc=(pe if k == DC - 1 else None))
            stG.consumed(h, (pe, pe.n))
            p_g = pe.n
            wts = [stU.loaded[s]] + psfree(ui)
            for k in range(DC):
                kw = wts if k == 0 else []
                if h == 0:
                    kw = list(kw) + [(dv, nmarks[k // 4])]
                em.op("tensor", "matmul", PS[ui][:], wus(s)[:, k, :], lnc(k),
                      start=(k == 0), stop=(k == DC - 1),
                      waits=kw,
                      inc=(pe if k == DC - 1 else None))
            stU.consumed(h, (pe, pe.n))
            p_u = pe.n
            if h + 3 < HC:
                stG.issue(h + 3, gsrc(h + 3), cached)
                stU.issue(h + 3, usrc(h + 3), cached)
            if h == 36:
                stD.issue(0, dsrc(0), cached)
                stD.issue(1, dsrc(1), cached)
            em.op("vector", "tensor_tensor", tg_s[:], PS[gi][:], s128n[:],
                  ALU.add,
                  waits=[(pe, p_g), (dv, s_mark), (ac, war["a_sg"])], inc=dv)
            set_psfree(gi, dv, dv.n)
            v_tg = dv.n
            s2 = h % 2
            wsg = [(dv, v_tg)]
            if sgH_mark[s2]:
                wsg.append((dv, sgH_mark[s2]))
            em.op("scalar", "activation", sg_scr[:, s2, :], tg_s[:], AF.Silu,
                  scale=sg_sb[:, h:h + 1], waits=wsg, inc=ac)
            a_sg = ac.n
            war["a_sg"] = a_sg
            em.op("vector", "tensor_tensor", tu_s[:], PS[ui][:], s128n[:],
                  ALU.add, waits=[(pe, p_u)], inc=dv)
            set_psfree(ui, dv, dv.n)
            em.wait("vector", dv, dv.n)
            wl = list(freeH) if first_H else []
            first_H = False
            freeH.clear()
            em.op("vector", "scalar_tensor_tensor", Hchunk(h), tu_s[:],
                  su_sb[:, h:h + 1], sg_scr[:, s2, :], ALU.mult, ALU.mult,
                  waits=[(ac, a_sg)] + wl, inc=dv)
            h_last = dv.n
            sgH_mark[s2] = h_last
            sh_q.append((h, h_last))
            while len(sh_q) > 6:
                hh, hm = sh_q.pop(0)
                wts = [(dv, hm)] + (psfree(6) if hh == 0 else [])
                em.op("tensor", "matmul", PS[6][:], ones_bf[:], Hchunk(hh),
                      start=(hh == 0), stop=(hh == HC - 1), waits=wts, inc=pe)
        while sh_q:
            hh, hm = sh_q.pop(0)
            wts = [(dv, hm)] + (psfree(6) if hh == 0 else [])
            em.op("tensor", "matmul", PS[6][:], ones_bf[:], Hchunk(hh),
                  start=(hh == 0), stop=(hh == HC - 1), waits=wts, inc=pe)
        p_sh = pe.n
        em.op("vector", "tensor_scalar_mul", shn[:], PS[6][:], -128.0,
              waits=[(pe, p_sh)], inc=dv)
        set_psfree(6, dv, dv.n)
        v_shn = dv.n
        em.phase = f"ffn{call}-down"
        xsrc = xin if call == 0 else x2h
        pre_mark = None
        stats_q = []
        for d in range(DC):
            sl = d % 2
            wxd = [(std[sl], std[sl].n)]
            if xd_free[sl][0]:
                wxd += [(dv, xd_free[sl][0]), (ac, xd_free[sl][1])]
            em.op("sync", "dma_start", out=xd_s[:, sl, :],
                  in_=xsrc[d * 128:(d + 1) * 128, t0:t0 + MT],
                  waits=wxd, inc=sxd[sl], incn=16)
            x_ld = sxd[sl].n
            pi = d % 4
            for j in range(HC):
                wts = []
                if j == 0:
                    wts = psfree(pi) + [stD.loaded[d % 2], (dv, h_last)]
                em.op("tensor", "matmul", PS[pi][:],
                      wds(d % 2)[:, j // 22, j % 22, :],
                      Hchunk(j), start=(j == 0), stop=(j == HC - 1),
                      waits=wts, inc=(pe if j == HC - 1 else None))
            p_d = pe.n
            while len(stats_q) > 1:
                dd, ssl, vb, va = stats_q.pop(0)
                wts = [(dv, vb)] + (psfree(4) if dd == 0 else [])
                em.op("tensor", "matmul", PS[4][:], ones_bf[:],
                      XBG[:, dd, :], start=(dd == 0), stop=(dd == DC - 1),
                      waits=wts, inc=pe)
                wts = [(ac, va)] + (psfree(5) if dd == 0 else [])
                em.op("tensor", "matmul", PS[5][:], ones_bf[:],
                      SQ[:, ssl, :], start=(dd == 0), stop=(dd == DC - 1),
                      waits=wts, inc=pe)
                sq_war[ssl] = pe.n
            if d + 2 < DC:
                stD.consumed(d, (pe, p_d))
                stD.issue(d + 2, dsrc(d + 2), cached)
            em.op("vector", "tensor_tensor", t1_s[:], PS[pi][:], shn[:],
                  ALU.add, waits=[(pe, p_d), (dv, v_shn)], inc=dv)
            set_psfree(pi, dv, dv.n)
            em.op("vector", "scalar_tensor_tensor", xd_s[:, sl, :], t1_s[:],
                  sdh_sb[:, d:d + 1], xd_s[:, sl, :], ALU.mult, ALU.add,
                  waits=[(sxd[sl], x_ld)], inc=dv)
            v_x1 = dv.n
            em.op("sync", "dma_start",
                  out=dst[d * 128:(d + 1) * 128, t0:t0 + MT],
                  in_=xd_s[:, sl, :], waits=[(dv, v_x1)], inc=std[sl], incn=16)
            if call == 0:
                wpc = [(pe, sq_war[sl])] if sq_war[sl] else []
                if d == 0:
                    wpc.append((pe, xbg_rd["pe"]))
                em.op("vector", "tensor_copy", XBG[:, d, :],
                      xd_s[:, sl, :], waits=wpc, inc=dv)
                vb = dv.n
                em.op("scalar", "activation", SQ[:, sl, :],
                      xd_s[:, sl, :], AF.Square,
                      waits=[(dv, v_x1)] + wpc, inc=ac)
                va = ac.n
                stats_q.append((d, sl, vb, va))
                xd_free[sl] = (vb, va)
        while stats_q:
            dd, ssl, vb, va = stats_q.pop(0)
            wts = [(dv, vb)] + (psfree(4) if dd == 0 else [])
            em.op("tensor", "matmul", PS[4][:], ones_bf[:],
                  XBG[:, dd, :], start=(dd == 0), stop=(dd == DC - 1),
                  waits=wts, inc=pe)
            wts = [(ac, va)] + (psfree(5) if dd == 0 else [])
            em.op("tensor", "matmul", PS[5][:], ones_bf[:],
                  SQ[:, ssl, :], start=(dd == 0), stop=(dd == DC - 1),
                  waits=wts, inc=pe)
            sq_war[ssl] = pe.n
        if call == 0:
            pre_mark = pe.n
        freeH.append((pe, pe.n))
        freeL.append((pe, pe.n))
        return pre_mark

    ffn.sgH = [0, 0]
    stats_pre = [None]
    xd_free = [(0, 0), (0, 0)]

    scan = {"o_ctr": 0, "qk_ctr": 0, "dd_free": [0, 0], "qs_free": [0, 0],
            "qkm_free": [0, 0], "kT_free": [0, 0], "tr_free": [0, 0],
            "o_free": [0, 0], "oS_free": [0, 0], "Lb_free": [0, 0],
            "Lb_mark": [0, 0]}
    ga_free = [0, 0]
    qg_st_free = [0, 0]

    def attn1(mt, pre):
        attn1.lmark = getattr(attn1, "lmark", [0] * NH)
        t0 = mt * MT
        cached = mt >= 1
        agate = (swb, wbmarks["attn"]) if cached else None
        pmark = (pe, pe.n)
        nmarks, _ = layernorm(mt, x1h, False, pre)
        ln_done = nmarks[3]
        stQ = WStream(swA, sfA, wgs, pmark, nslots=3, gate=agate)
        qk_list = []
        for h in range(NH):
            qk_list.append((3 * h, True, h))
            qk_list.append((3 * h + 1, False, h))

        def qsrc(o):
            return (wqkvb if cached else wqkv)[o].rearrange("p c o -> p c o")

        def qwb(o):
            return None if cached else wqkvb[o].rearrange("p c o -> p c o")

        for i0 in range(3):
            stQ.issue(i0, qsrc(qk_list[i0][0]), cached,
                      wb_dst=qwb(qk_list[i0][0]))
        em.phase = "qkproj"
        firstH = True
        for i, (o, is_q, h) in enumerate(qk_list):
            s = i % 3
            pi = i % 4
            wts = [stQ.loaded[s]] + psfree(pi)
            if i > 0:
                wts.append((dv, ln_done))
            for k in range(DC):
                kw = wts if k == 0 else []
                if i == 0:
                    kw = list(kw) + [(dv, nmarks[k // 4])]
                em.op("tensor", "matmul", PS[pi][:], wgs(s)[:, k, :], lnc(k),
                      start=(k == 0), stop=(k == DC - 1),
                      waits=kw,
                      inc=(pe if k == DC - 1 else None))
            stQ.consumed(i, (pe, pe.n))
            p_mm = pe.n
            if i + 3 < len(qk_list):
                stQ.issue(i + 3, qsrc(qk_list[i + 3][0]), cached, bg=3,
                          wb_dst=qwb(qk_list[i + 3][0]))
            wts = [(pe, p_mm)]
            if firstH:
                wts += freeH
                freeH.clear()
                firstH = False
            em.op("scalar", "activation", qv(h) if is_q else kv_(h),
                  PS[pi][:], AF.Silu, waits=wts, inc=ac)
            set_psfree(pi, ac, ac.n)
        vload = [None, None]
        vfree = [pmark, pmark]
        v_wb = [(swb, swb.n) if swb.n else None] * 2

        def v_issue(g):
            s = g % 2
            w = [vfree[s]] if vfree[s] else []
            if v_wb[s] is not None:
                w.append(v_wb[s])
                v_wb[s] = None
            if cached:
                w.append(agate)
                em.op("scalar", "dma_start", out=wvs(s)[:, :, :],
                      in_=wvb[g], waits=w, inc=sfB[s], incn=16)
                vload[s] = (sfB[s], sfB[s].n)
            else:
                em.op("gpsimd", "dma_start", out=wvs(s)[:, :, :],
                      in_=wvin[g], waits=w, inc=swB[s], incn=16)
                vload[s] = (swB[s], swB[s].n)
                em.op("sync", "dma_start", out=wvb[g],
                      in_=wvs(s)[:, :, :], waits=[vload[s]],
                      inc=swb, incn=16)
                v_wb[s] = (swb, swb.n)
            bg_emit(3)

        em.phase = "vproj"
        v_issue(0)
        v_issue(1)
        firstX = True
        for g in range(NH // 2):
            s = g % 2
            for tc in range(TC):
                pi = (g * TC + tc) % 4
                wts = [vload[s]] + psfree(pi)
                for k in range(DC):
                    em.op("tensor", "matmul", PS[pi][:, 0:256],
                          lnc(k)[:, tc * 128:(tc + 1) * 128], wvs(s)[:, k, :],
                          start=(k == 0), stop=(k == DC - 1),
                          waits=wts if k == 0 else (),
                          inc=(pe if k == DC - 1 else None))
                p_v = pe.n
                wts2 = [(pe, p_v)]
                if firstX:
                    wts2 += freeX
                    freeX.clear()
                    firstX = False
                em.op("scalar", "activation",
                      vT(tc)[:, g * 256:(g + 1) * 256], PS[pi][:, 0:256],
                      AF.Silu, waits=wts2, inc=ac)
                set_psfree(pi, ac, ac.n)
            vfree[s] = (pe, pe.n)
            if g + 2 < NH // 2:
                v_issue(g + 2)
        def gasrc(o):
            return (wgab if cached else wga)[o]

        def gawb(o):
            return None if cached else wgab[o]

        em.phase = "gateproj"
        nq = len(qk_list)
        stQ.issue(nq, gasrc(0), cached, wb_dst=gawb(0))
        stQ.issue(nq + 1, gasrc(1), cached, wb_dst=gawb(1))
        stQ.issue(nq + 2, gasrc(2), cached, wb_dst=gawb(2))
        for o in range(DC):
            s = (nq + o) % 3
            pi = o % 4
            wts = [stQ.loaded[s], (dv, ln_done)] + psfree(pi)
            for k in range(DC):
                em.op("tensor", "matmul", PS[pi][:], wgs(s)[:, k, :], lnc(k),
                      start=(k == 0), stop=(k == DC - 1),
                      waits=wts if k == 0 else (),
                      inc=(pe if k == DC - 1 else None))
            stQ.consumed(nq + o, (pe, pe.n))
            p_mm = pe.n
            if o + 3 < DC:
                stQ.issue(nq + o + 3, gasrc(o + 3), cached, bg=3,
                          wb_dst=gawb(o + 3))
            gs = o % 2
            w2 = [(pe, p_mm)]
            if ga_free[gs]:
                w2.append((stg[gs], ga_free[gs]))
            em.op("scalar", "activation", ga_s[:, gs, :], PS[pi][:],
                  AF.Sigmoid, waits=w2, inc=ac)
            set_psfree(pi, ac, ac.n)
            a_g = ac.n
            em.op("sync", "dma_start",
                  out=gah[o * 128:(o + 1) * 128, t0:t0 + MT],
                  in_=ga_s[:, gs, :], waits=[(ac, a_g)], inc=stg[gs], incn=16)
            ga_free[gs] = stg[gs].n
        a_evac = ac.n
        em.wait("tensor", ac, a_evac)
        em.phase = "scan"
        lmark = attn1.lmark

        def build_dd(h):
            hs = h % 2
            wts = [(dv, setup.marks["dv"]), (ac, setup.marks["ac"]),
                   (dv, dv.n)]
            em.op("scalar", "activation",
                  ddT[:, hs].rearrange("p a b -> p (a b)"),
                  diffc[:].rearrange("p a b -> p (a b)"), AF.Exp,
                  scale=snegrep[:, h:h + 1], waits=wts, inc=ac)
            v_dd = ac.n
            em.op("scalar", "activation", qdec[:, hs, :], mbase[:], AF.Exp,
                  scale=snegrep[:, h:h + 1], inc=ac)
            return v_dd, ac.n

        def stage1(h, b, v_dd, a_qd):
            hs = h % 2
            blkg = mt * NB + b
            mcols = slice(b * BLK, (b + 1) * BLK)
            sc = scan["o_ctr"]
            scan["o_ctr"] += 1
            if blkg > 0 and b == 0:
                wts = []
                if scan["Lb_free"][hs]:
                    wts.append((pe, scan["Lb_free"][hs]))
                em.op("vector", "tensor_copy", Lbf[:, hs, :], L_sb[:, h, :],
                      waits=wts, inc=dv)
                scan["Lb_mark"][hs] = dv.n
            wts = [(ac, a_qd), (ac, a_evac)]
            if scan["qs_free"][hs]:
                wts.append((pe, scan["qs_free"][hs]))
            em.op("vector", "tensor_mul", qs_s[:, hs, :], qv(h)[:, mcols],
                  qdec[:, hs, :], waits=wts, inc=dv)
            v_qs = dv.n
            em.wait("vector", dv, v_qs)
            qgs = h % 2
            wts = [(stq[qgs], qg_st_free[qgs])] if (qg_st_free[qgs] and b == 0) else []
            em.op("vector", "tensor_scalar_mul",
                  qg_s[:, qgs, b * BLK:(b + 1) * BLK], qs_s[:, hs, :],
                  Db[:, blkg, h:h + 1], waits=wts, inc=dv)
            v_qg = dv.n
            if b == NB - 1:
                em.op("sync", "dma_start",
                      out=qgh[h * 128:(h + 1) * 128, t0:t0 + MT],
                      in_=qg_s[:, qgs, :], waits=[(dv, v_qg)],
                      inc=stq[qgs], incn=16)
                qg_st_free[qgs] = stq[qgs].n
            qkpi = sc % 2
            wts = psfree(qkpi)
            for ncx in range(2):
                em.op("tensor", "matmul",
                      PS[qkpi][:, ncx * 256:(ncx + 1) * 256],
                      kv_(h)[:, b * BLK + ncx * 128:b * BLK + (ncx + 1) * 128],
                      qv(h)[:, mcols], start=True, stop=True,
                      waits=wts if ncx == 0 else (), inc=pe)
            p_qk = pe.n
            trp = sc % 2
            trdst = PTR if trp == 0 else PS6B
            wts = [(gp, setup.marks["gp"])]
            if scan["tr_free"][trp]:
                wts.append((ac, scan["tr_free"][trp]))
            if trp == 1 and psum_free[6]:
                wts.append((psum_free_sem[6], psum_free[6]))
                psum_free[6] = 0
            for ncx in range(2):
                em.op("tensor", "transpose",
                      trdst[:, ncx * 128:(ncx + 1) * 128],
                      kv_(h)[:, b * BLK + ncx * 128:b * BLK + (ncx + 1) * 128],
                      ident[:], waits=wts if ncx == 0 else (), inc=pe)
            p_tr = pe.n
            cpi = 2 + sc % 2
            n_o = 0
            if blkg > 0:
                wts = [(dv, scan["Lb_mark"][hs]), (dv, v_qs)] + psfree(cpi)
                em.op("tensor", "matmul", PS[cpi][:, 256:512],
                      Lbf[:, hs, :], qs_s[:, hs, :], start=True, stop=False,
                      waits=wts, inc=pe)
                n_o = 1
                scan["Lb_free"][hs] = pe.n
            scan["qs_free"][hs] = pe.n
            qkms = sc % 2
            wts = [(pe, p_qk), (ac, v_dd)]
            if scan["qkm_free"][qkms]:
                wts.append((pe, scan["qkm_free"][qkms]))
            em.op("vector", "tensor_tensor",
                  qkm_s[:, qkms, :, :].rearrange("p a b -> p (a b)"),
                  PS[qkpi][:],
                  ddT[:, hs].rearrange("p a b -> p (a b)"),
                  ALU.mult, waits=wts, inc=dv)
            v_qkm = dv.n
            set_psfree(qkpi, dv, v_qkm)
            wts = [(pe, p_tr)]
            if scan["kT_free"][trp]:
                wts.append((pe, scan["kT_free"][trp]))
            for ncx in range(2):
                em.op("scalar", "activation", kT_s[:, trp, ncx, :],
                      trdst[:, ncx * 128:(ncx + 1) * 128],
                      AF.Identity, scale=kdecT[:, h, ncx:ncx + 1],
                      waits=wts if ncx == 0 else (), inc=ac)
            a_kT = ac.n
            scan["tr_free"][trp] = a_kT
            return dict(sc=sc, v_qkm=v_qkm, a_kT=a_kT, n_o=n_o, h=h, b=b)

        def stage2(st):
            h, b = st["h"], st["b"]
            hs = h % 2
            sc = st["sc"]
            qkms = sc % 2
            trp = sc % 2
            cpi = 2 + sc % 2
            trdst = PTR if trp == 0 else PS6B
            oreg = PS[cpi][:, 256:512]
            for ncx in range(2):
                tc = b * 2 + ncx
                wts = []
                if ncx == 0:
                    wts = [(dv, st["v_qkm"])]
                    if st["n_o"] == 0:
                        wts += psfree(cpi)
                em.op("tensor", "matmul", oreg,
                      vT(tc)[:, h * 128:(h + 1) * 128],
                      qkm_s[:, qkms, ncx, :],
                      start=(st["n_o"] == 0 and ncx == 0), stop=(ncx == 1),
                      waits=wts, inc=pe)
            p_o = pe.n
            scan["qkm_free"][qkms] = p_o
            wts = [(ac, st["a_kT"])]
            for ncx in range(2):
                tc = b * 2 + ncx
                em.op("tensor", "matmul", PS[cpi][:, 0:128],
                      kT_s[:, trp, ncx, :],
                      vT(tc)[:, h * 128:(h + 1) * 128],
                      start=(ncx == 0), stop=(ncx == 1),
                      waits=wts if ncx == 0 else (), inc=pe)
            p_c = pe.n
            scan["kT_free"][trp] = p_c
            em.op("vector", "scalar_tensor_tensor", L_sb[:, h, :],
                  L_sb[:, h, :], Db[:, 1, h:h + 1], PS[cpi][:, 0:128],
                  ALU.mult, ALU.add, waits=[(pe, p_c)], inc=dv)
            set_psfree(cpi, dv, dv.n)
            lmark[h] = dv.n
            em.wait("vector", dv, dv.n)
            if b + 1 < NB:
                wts = []
                if scan["Lb_free"][hs]:
                    wts.append((pe, scan["Lb_free"][hs]))
                em.op("vector", "tensor_copy", Lbf[:, hs, :], L_sb[:, h, :],
                      waits=wts, inc=dv)
                scan["Lb_mark"][hs] = dv.n
            osl = h % 2
            wts = [(pe, p_o)]
            if scan["oS_free"][osl] and b == 0:
                wts.append((sto[osl], scan["oS_free"][osl]))
            em.op("vector", "tensor_copy",
                  o_s[:, osl, b * BLK:(b + 1) * BLK], oreg,
                  waits=wts, inc=dv)
            set_psfree(cpi, dv, dv.n)
            v_oe = dv.n
            if b == NB - 1:
                em.op("sync", "dma_start",
                      out=oh[h * 128:(h + 1) * 128, t0:t0 + MT],
                      in_=o_s[:, osl, :], waits=[(dv, v_oe)],
                      inc=sto[osl], incn=16)
                scan["oS_free"][osl] = sto[osl].n

        def ln1_prefetch(k):
            sl = k % 2
            t1n = (mt + 1) * MT
            wxd = [(std[sl], std[sl].n)]
            if xd_free[sl][0]:
                wxd += [(dv, xd_free[sl][0]), (ac, xd_free[sl][1])]
            em.op("sync", "dma_start", out=xd_s[:, sl, :],
                  in_=xin[k * 128:(k + 1) * 128, t1n:t1n + MT],
                  waits=wxd, inc=sxd[sl], incn=16)
            x_ld = sxd[sl].n
            w0 = [(sxd[sl], x_ld)]
            if k == 0:
                w0.append((pe, xbg_rd["pe"]))
            em.op("vector", "tensor_copy", XBG[:, k, :], xd_s[:, sl, :],
                  waits=w0, inc=dv)
            vb = dv.n
            wq = [(sxd[sl], x_ld)]
            if sq_war[sl]:
                wq.append((pe, sq_war[sl]))
            em.op("scalar", "activation", SQ[:, sl, :], xd_s[:, sl, :],
                  AF.Square, waits=wq, inc=ac)
            va = ac.n
            wts = [(dv, vb)] + (psfree(4) if k == 0 else [])
            em.op("tensor", "matmul", PS[4][:], ones_bf[:], XBG[:, k, :],
                  start=(k == 0), stop=(k == DC - 1), waits=wts, inc=pe)
            wts = [(ac, va)] + (psfree(5) if k == 0 else [])
            em.op("tensor", "matmul", PS[5][:], ones_bf[:], SQ[:, sl, :],
                  start=(k == 0), stop=(k == DC - 1), waits=wts, inc=pe)
            sq_war[sl] = pe.n
            xd_free[sl] = (vb, va)

        for g in range(NH // 2):
            hA, hB = 2 * g, 2 * g + 1
            ddA = build_dd(hA)
            ddB = build_dd(hB)
            for b in range(NB):
                sA = stage1(hA, b, *ddA)
                sB = stage1(hB, b, *ddB)
                stage2(sA)
                stage2(sB)
            if mt < NMT - 1:
                ln1_prefetch(hA)
                ln1_prefetch(hB)
        scan["dd_free"][0] = dv.n
        scan["dd_free"][1] = dv.n
        if scan["tr_free"][1]:
            set_psfree(6, ac, scan["tr_free"][1])
        if mt < NMT - 1:
            stats_pre[0] = pe.n
        freeH.append((pe, pe.n))
        freeH.append((dv, dv.n))
        freeL.append((pe, pe.n))
        freeX.append((pe, pe.n))
        set_psfree(6, dv, max(scan["o_free"]))
        if mt == 0:
            wbmarks["attn"] = swb.n
        if mt == 1:
            bg_emit(10 ** 9)

    def cc_phase():
        v_all = dv.n
        p_all = pe.n
        bg_emit(10 ** 9)
        em.op("gpsimd", "dma_start",
              out=cci.rearrange("(h d) e -> d h e", d=128),
              in_=L_sb[:],
              waits=[(dv, v_all)], inc=scc, incn=16)
        em.wait("gpsimd", scc, scc.n)
        em.op("gpsimd", "collective_compute", "AllGather", ALU.bypass,
              replica_groups=[[0, 1, 2, 3], [4, 5, 6, 7]],
              ins=[cci], outs=[cco], inc=ccd)
        wH = list(freeH)
        freeH.clear()
        pf = {}
        em.wait("sync", sto[0], sto[0].n)
        em.wait("sync", sto[1], sto[1].n)
        em.wait("sync", stq[0], stq[0].n)
        em.wait("sync", stq[1], stq[1].n)
        em.wait("sync", stg[0], stg[0].n)
        em.wait("sync", stg[1], stg[1].n)
        em.op("sync", "dma_start",
              out=H_f32[:, 0:8192].rearrange("p (h t) -> p h t", h=NH),
              in_=oh[:, 0:MT].rearrange("(h p) t -> p h t", p=128),
              waits=wH, inc=sol, incn=16)
        pf["o_ld"] = sol.n
        em.op("sync", "dma_start",
              out=AR_L[:].rearrange("p (h t) -> p h t", h=NH),
              in_=gah[:, 0:MT].rearrange("(h p) t -> p h t", p=128),
              waits=list(freeL), inc=sgl, incn=16)
        freeL.clear()
        pf["g_ld"] = sgl.n
        for hh in range(2):
            em.op("sync", "dma_start", out=qgl_s[:, hh, :],
                  in_=qgh[hh * 128:(hh + 1) * 128, 0:MT],
                  inc=sqg[hh], incn=16)
        pf["qg_ld"] = [sqg[0].n, sqg[1].n]
        em.wait("gpsimd", ccd, 1)
        Tj = [H_f32[:, 8192:10240].rearrange("p (h e) -> p h e", h=NH),
              H_f32[:, 8192:10240].rearrange("p (h e) -> p h e", h=NH)]
        Sst = H_f32[:, 10240:12288]
        tmp = H_f32[:, 12288:14336]
        em.op("vector", "memset", Sst, 0.0, waits=wH, inc=dv)
        v_z = dv.n
        mult_mark = v_z
        for j in range(4):
            ssem = scc if j % 2 == 0 else scc2
            wts = [(pe, p_all), (dv, mult_mark)]
            em.op("gpsimd", "dma_start", out=Tj[0][:, :, :],
                  in_=cco[j * 2048:(j + 1) * 2048].rearrange(
                      "(h d) e -> d h e", d=128),
                  waits=wts, inc=ssem, incn=16)
            ld_mark = (ssem, ssem.n)
            wbc = wrep[:, j, :].unsqueeze(2).broadcast_to([128, NH, HD])
            em.op("vector", "tensor_tensor",
                  tmp.rearrange("p (h e) -> p h e", h=NH),
                  Tj[0][:, :, :], wbc, ALU.mult,
                  waits=[ld_mark, (dv, setup.marks["dv"]),
                         setup.marks["wrep"]], inc=dv)
            mult_mark = dv.n
            em.op("vector", "tensor_tensor", Sst, Sst, tmp, ALU.add, inc=dv)
        last = dv.n
        em.op("vector", "tensor_copy", Ssbf.rearrange("p h e -> p (h e)"),
              Sst, waits=[(dv, last)], inc=dv)
        freeH.append((dv, dv.n))
        return dv.n, pf

    qg_free = [0, 0]
    var_mm_last = [0]

    def pass2(mt, ss_mark, pf=None):
        t0 = mt * MT
        if pf is None:
            em.wait("sync", sto[0], sto[0].n)
            em.wait("sync", sto[1], sto[1].n)
            em.wait("sync", stq[0], stq[0].n)
            em.wait("sync", stq[1], stq[1].n)
            em.wait("sync", stg[0], stg[0].n)
            em.wait("sync", stg[1], stg[1].n)
            wts = list(freeH)
            freeH.clear()
            em.op("sync", "dma_start",
                  out=H_f32[:, 0:8192].rearrange("p (h t) -> p h t", h=NH),
                  in_=oh[:, t0:t0 + MT].rearrange("(h p) t -> p h t", p=128),
                  waits=wts, inc=sol, incn=16)
            o_ld = sol.n
            wts = list(freeL)
            freeL.clear()
            em.op("sync", "dma_start",
                  out=AR_L[:].rearrange("p (h t) -> p h t", h=NH),
                  in_=gah[:, t0:t0 + MT].rearrange("(h p) t -> p h t", p=128),
                  waits=wts, inc=sgl, incn=16)
            g_ld = sgl.n
            qg_ld = [0, 0]
        else:
            o_ld = pf["o_ld"]
            g_ld = pf["g_ld"]
            qg_ld = list(pf["qg_ld"])
            freeH.clear()
            freeL.clear()

        def qg_issue(h):
            s = h % 2
            w = [(pe, qg_free[s])] if qg_free[s] else []
            em.op("sync", "dma_start", out=qgl_s[:, s, :],
                  in_=qgh[h * 128:(h + 1) * 128, t0:t0 + MT],
                  waits=w, inc=sqg[s], incn=16)
            qg_ld[s] = sqg[s].n

        if pf is None:
            qg_issue(0)
            qg_issue(1)
        em.phase = "pass2-oc"
        for h in range(NH):
            s = h % 2
            pi = s
            wts = [(sqg[s], qg_ld[s]), (dv, ss_mark)] + psfree(pi)
            em.op("tensor", "matmul", PS[pi][:], Ssbf[:, h, :], qgl_s[:, s, :],
                  start=True, stop=True, waits=wts, inc=pe)
            p_oc = pe.n
            qg_free[s] = p_oc
            if h + 2 < NH:
                qg_issue(h + 2)
            em.op("vector", "tensor_tensor", ofull(h), ofull(h), PS[pi][:],
                  ALU.add, waits=[(pe, p_oc), (sol, o_ld)], inc=dv)
            set_psfree(pi, dv, dv.n)
            em.wait("vector", dv, dv.n)
            v_of = dv.n
            osl2 = h % 2
            wsq = [(dv, v_of), (pe, var_mm_last[0])]
            if sq_war[osl2]:
                wsq.append((pe, sq_war[osl2]))
            em.op("scalar", "activation", SQ[:, osl2, :], ofull(h), AF.Square,
                  waits=wsq, inc=ac)
            a_sq = ac.n
            wts = [(ac, a_sq)] + (psfree(4) if h == 0 else [])
            em.op("tensor", "matmul", PS[4][:], ones_bf[:], SQ[:, osl2, :],
                  start=(h == 0), stop=(h == NH - 1), waits=wts, inc=pe)
            var_mm_last[0] = pe.n
            sq_war[osl2] = pe.n
        p_var = pe.n
        em.op("vector", "tensor_scalar", R2[:], PS[4][:], 1.0 / D, EPS_RMS,
              ALU.mult, ALU.add, waits=[(pe, p_var)], inc=dv)
        set_psfree(4, dv, dv.n)
        em.wait("vector", dv, dv.n)
        em.op("vector", "reciprocal", R2[:], R2[:], inc=dv)
        v_r2 = dv.n
        em.op("scalar", "activation", R2[:], R2[:], AF.Sqrt,
              waits=[(dv, v_r2)], inc=ac)
        a_R = ac.n
        z_marks = []
        for h in range(NH):
            em.op("vector", "scalar_tensor_tensor", t1_s[:], ofull(h),
                  nw_sb[:, h:h + 1], R2[:], ALU.mult, ALU.mult,
                  waits=[(ac, a_R)], inc=dv)
            em.wait("vector", dv, dv.n)
            em.op("vector", "tensor_tensor", gal(h), t1_s[:], gal(h),
                  ALU.mult, waits=[(sgl, g_ld)], inc=dv)
            z_marks.append(dv.n)
        v_z = dv.n
        freeH.append((dv, v_z))
        stW = WStream(swA, sfA, wgs, (pe, pe.n), nslots=3)

        def osrc(f):
            return wob[f]

        em.phase = "outproj"
        stW.issue(0, osrc(0), True)
        stW.issue(1, osrc(1), True)
        stW.issue(2, osrc(2), True)
        for f in range(DC):
            s = f % 3
            pi = f % 4
            wts = [stW.loaded[s]] + psfree(pi)
            for k in range(DC):
                kw = list(wts) if k == 0 else []
                if f == 0:
                    kw.append((dv, z_marks[k]))
                em.op("tensor", "matmul", PS[pi][:], wgs(s)[:, k, :], gal(k),
                      start=(k == 0), stop=(k == DC - 1),
                      waits=kw,
                      inc=(pe if k == DC - 1 else None))
            stW.consumed(f, (pe, pe.n))
            p_mm = pe.n
            if f + 3 < DC:
                stW.issue(f + 3, osrc(f + 3), True)
            sl = f % 2
            wxd = [(std[sl], std[sl].n)]
            if xd_free[sl][0]:
                wxd += [(dv, xd_free[sl][0]), (ac, xd_free[sl][1])]
            em.op("sync", "dma_start", out=xd_s[:, sl, :],
                  in_=x1h[f * 128:(f + 1) * 128, t0:t0 + MT],
                  waits=wxd, inc=sxd[sl], incn=16)
            x_ld = sxd[sl].n
            em.op("vector", "tensor_tensor", xd_s[:, sl, :], PS[pi][:],
                  xd_s[:, sl, :], ALU.add,
                  waits=[(pe, p_mm), (sxd[sl], x_ld)], inc=dv)
            set_psfree(pi, dv, dv.n)
            v_x2 = dv.n
            em.op("sync", "dma_start",
                  out=x2h[f * 128:(f + 1) * 128, t0:t0 + MT],
                  in_=xd_s[:, sl, :], waits=[(dv, v_x2)], inc=std[sl], incn=16)
            wpc = [(pe, sq_war[sl])] if sq_war[sl] else []
            if f == 0:
                wpc.append((pe, xbg_rd["pe"]))
            em.op("vector", "tensor_copy", XBG[:, f, :],
                  xd_s[:, sl, :], waits=wpc, inc=dv)
            vb = dv.n
            em.op("scalar", "activation", SQ[:, sl, :],
                  xd_s[:, sl, :], AF.Square, waits=[(dv, v_x2)] + wpc, inc=ac)
            va = ac.n
            wts = [(dv, vb)] + (psfree(4) if f == 0 else [])
            em.op("tensor", "matmul", PS[4][:], ones_bf[:],
                  XBG[:, f, :], start=(f == 0), stop=(f == DC - 1),
                  waits=wts, inc=pe)
            wts = [(ac, va)] + (psfree(5) if f == 0 else [])
            em.op("tensor", "matmul", PS[5][:], ones_bf[:],
                  SQ[:, sl, :], start=(f == 0), stop=(f == DC - 1),
                  waits=wts, inc=pe)
            sq_war[sl] = pe.n
            xd_free[sl] = (vb, va)
        stats_pre[0] = pe.n
        freeL.append((pe, pe.n))

    for pair in ((wob, wo, DC), (wgb, wg, HC), (wub, wu, HC)):
        dst_t, src_t, n = pair
        for i in range(n):
            bg_jobs.append((dst_t[i], src_t[i]))
    for i in range(DC):
        bg_jobs.append((wdb[i], wd[i]))

    setup()
    for mt in range(NMT):
        pre = ffn(mt, 0, x1h)
        attn1(mt, pre)
    ss_mark, pf0 = cc_phase()
    for mt in range(NMT):
        pass2(mt, ss_mark, pf0 if mt == 0 else None)
        ffn(mt, 1, out)
    em.wait("sync", std[0], std[0].n)
    em.wait("sync", std[1], std[1].n)

    with nc.Block() as block:
        @block.sync
        def _(e):
            em.replay("sync", e)

        @block.gpsimd
        def _(e):
            em.replay("gpsimd", e)

        @block.vector
        def _(e):
            em.replay("vector", e)

        @block.scalar
        def _(e):
            em.replay("scalar", e)

        @block.tensor
        def _(e):
            em.replay("tensor", e)

    es.close()
    nc._em_labels = em.mm_labels
    return nc


_NC_CACHE = []


def _stage_lhsT(w):
    w = np.asarray(w)
    O, K = w.shape
    wT = np.ascontiguousarray(w.T)
    v = wT.reshape(K // 128, 128, O // 128, 128)
    return np.ascontiguousarray(v.transpose(2, 1, 0, 3))


def kernel(L0_attn_hidden_states, L0_attn_slope_rate, L0_attn_w_qkv,
           L0_attn_w_gate, L0_attn_w_out, L0_attn_norm_w,
           L0_ffn_wq_gate, L0_ffn_s_gate, L0_ffn_wq_up, L0_ffn_s_up,
           L0_ffn_wq_down, L0_ffn_s_down):
    x = np.asarray(L0_attn_hidden_states)
    xs = np.ascontiguousarray(x.reshape(16384, D))

    wg_s = _stage_lhsT(L0_ffn_wq_gate)
    wu_s = _stage_lhsT(L0_ffn_wq_up)
    wdT = np.ascontiguousarray(np.asarray(L0_ffn_wq_down).T)
    v = wdT.reshape(2, 22, 128, DC, 128)
    wd_s = np.ascontiguousarray(v.transpose(3, 0, 2, 1, 4))
    wqkv_s = _stage_lhsT(L0_attn_w_qkv)
    _vb = wqkv_s[2::3]
    wv_s = np.ascontiguousarray(np.concatenate([_vb[0::2], _vb[1::2]],
                                               axis=-1))
    wga_s = _stage_lhsT(L0_attn_w_gate)
    wo_s = _stage_lhsT(L0_attn_w_out)
    sg_s = np.ascontiguousarray(
        np.asarray(L0_ffn_s_gate).reshape(HC, 128).T.astype(np.float32))
    su_s = np.ascontiguousarray(
        np.asarray(L0_ffn_s_up).reshape(HC, 128).T.astype(np.float32))
    sd_s = np.ascontiguousarray(
        np.asarray(L0_ffn_s_down).reshape(DC, 128).T.astype(np.float32))
    nw_s = np.ascontiguousarray(
        np.asarray(L0_attn_norm_w).reshape(DC, 128).T.astype(np.float32))
    sl_s = np.ascontiguousarray(
        np.asarray(L0_attn_slope_rate).reshape(1, NH).astype(np.float32))

    if not _NC_CACHE:
        _NC_CACHE.append(build_program())
    nc = _NC_CACHE[0]
    in_maps = []
    for c in range(NCORES):
        g = c % 4
        ccm_c = np.zeros((1, 4), np.float32)
        ccp_c = np.zeros((1, 4), np.float32)
        for j in range(4):
            if j < g:
                ccm_c[0, j] = 1.0
                ccp_c[0, j] = -2048.0 * (g - 1 - j)
        xT = np.ascontiguousarray(xs[c * T:(c + 1) * T].T)
        in_maps.append({
            "xin": xT, "wg": wg_s, "wu": wu_s, "wd": wd_s,
            "wqkv": wqkv_s, "wvin": wv_s, "wga": wga_s, "wo": wo_s,
            "sgate": sg_s, "sup": su_s, "sdown": sd_s, "nwin": nw_s,
            "slopes": sl_s, "ccm": ccm_c, "ccp": ccp_c,
        })
    res = run_bass_kernel_spmd(nc, in_maps, core_ids=list(range(NCORES)))
    outs = [np.asarray(res.results[c]["out"]).T for c in range(NCORES)]
    full = np.concatenate(outs, axis=0).reshape(2, 8192, D)
    return full.astype(np.float32)


# version 25 (final)
# speedup vs baseline: 1.0218x; 1.0218x over previous
import numpy as np
from contextlib import ExitStack
import concourse.bass as bass
import concourse.mybir as mybir
from concourse.bass_utils import run_bass_kernel_spmd

F32 = mybir.dt.float32
BF16 = mybir.dt.bfloat16
I32 = mybir.dt.int32
AF = mybir.ActivationFunctionType
ALU = mybir.AluOpType

NCORES = 8
T = 2048
D = 2048
DC = 16
HID = 5632
HC = 44
NH = 16
HD = 128
BLK = 256
MT = 512
NMT = T // MT
TC = MT // 128
NB = MT // BLK
EPS_LN = 1e-5
EPS_RMS = 1e-6


class Sem:
    def __init__(self, nc, es, name):
        self.h = es.enter_context(nc.semaphore(name))
        self.n = 0
        self.name = name

    def bump(self, k):
        self.n += k
        return self.n


class Em:
    ENGS = ("sync", "gpsimd", "vector", "scalar", "tensor")

    def __init__(self):
        self.q = {e: [] for e in self.ENGS}
        self.autosync = {}
        self._last_auto = {}
        self.mm_labels = []
        self.phase = "setup"

    def op(self, eng, meth, *a, waits=(), inc=None, incn=1, **kw):
        if eng == "tensor" and meth in ("matmul", "transpose"):
            self.mm_labels.append(self.phase)
        if eng in self.autosync:
            s0 = self.autosync[eng]
            if s0.n > 0 and self._last_auto.get(eng, -1) < s0.n:
                self.q[eng].append(("wait", s0, s0.n))
                self._last_auto[eng] = s0.n
        ws = [(s, v) for (s, v) in waits if v > 0]
        for (s, v) in ws:
            assert v <= s.n, f"wait on future: {s.name} {v} > {s.n} ({eng}.{meth})"
        inc.bump(incn) if inc is not None else None
        self.q[eng].append(("op", meth, a, kw, ws, inc, incn))

    def wait(self, eng, sem, val):
        if val <= 0:
            return
        assert val <= sem.n, f"wait on future: {sem.name} {val} > {sem.n}"
        self.q[eng].append(("wait", sem, val))

    def replay(self, eng_name, e):
        for item in self.q[eng_name]:
            if item[0] == "wait":
                _, sem, val = item
                e.wait_ge(sem.h, val)
            else:
                _, meth, a, kw, ws, inc, incn = item
                for (s, v) in ws:
                    e.wait_ge(s.h, v)
                instr = getattr(e, meth)(*a, **kw)
                if inc is not None:
                    instr.then_inc(inc.h, incn)


def build_program():
    nc = bass.Bass("TRN2", target_bir_lowering=False)
    es = ExitStack()

    xin = nc.dram_tensor("xin", [D, T], F32, kind="ExternalInput").ap()
    wg = nc.dram_tensor("wg", [HC, 128, DC, 128], I32, kind="ExternalInput").ap()
    wu = nc.dram_tensor("wu", [HC, 128, DC, 128], I32, kind="ExternalInput").ap()
    wd = nc.dram_tensor("wd", [DC, 2, 128, 22, 128], I32, kind="ExternalInput").ap()
    wqkv = nc.dram_tensor("wqkv", [48, 128, DC, 128], F32, kind="ExternalInput").ap()
    wvin = nc.dram_tensor("wvin", [8, 128, DC, 256], F32, kind="ExternalInput").ap()
    wga = nc.dram_tensor("wga", [DC, 128, DC, 128], F32, kind="ExternalInput").ap()
    wo = nc.dram_tensor("wo", [DC, 128, DC, 128], F32, kind="ExternalInput").ap()
    sgate = nc.dram_tensor("sgate", [128, HC], F32, kind="ExternalInput").ap()
    sup = nc.dram_tensor("sup", [128, HC], F32, kind="ExternalInput").ap()
    sdown = nc.dram_tensor("sdown", [128, DC], F32, kind="ExternalInput").ap()
    nwin = nc.dram_tensor("nwin", [128, DC], F32, kind="ExternalInput").ap()
    slopes = nc.dram_tensor("slopes", [1, NH], F32, kind="ExternalInput").ap()
    ccm = nc.dram_tensor("ccm", [1, 4], F32, kind="ExternalInput").ap()
    ccp = nc.dram_tensor("ccp", [1, 4], F32, kind="ExternalInput").ap()
    out = nc.dram_tensor("out", [D, T], F32, kind="ExternalOutput").ap()

    x1h = nc.dram_tensor("x1h", [D, T], F32, kind="Internal").ap()
    x2h = nc.dram_tensor("x2h", [D, T], F32, kind="Internal").ap()
    oh = nc.dram_tensor("oh", [D, T], F32, kind="Internal").ap()
    qgh = nc.dram_tensor("qgh", [D, T], BF16, kind="Internal").ap()
    gah = nc.dram_tensor("gah", [D, T], BF16, kind="Internal").ap()
    wgb = nc.dram_tensor("wgb", [HC, 128, DC, 128], BF16, kind="Internal").ap()
    wub = nc.dram_tensor("wub", [HC, 128, DC, 128], BF16, kind="Internal").ap()
    wdb = nc.dram_tensor("wdb", [DC, 2, 128, 22, 128], BF16, kind="Internal").ap()
    wqkvb = nc.dram_tensor("wqkvb", [48, 128, DC, 128], BF16, kind="Internal").ap()
    wvb = nc.dram_tensor("wvb", [8, 128, DC, 256], BF16, kind="Internal").ap()
    wgab = nc.dram_tensor("wgab", [DC, 128, DC, 128], BF16, kind="Internal").ap()
    wob = nc.dram_tensor("wob", [DC, 128, DC, 128], BF16, kind="Internal").ap()
    cci = nc.dram_tensor("cci", [NH * HD, HD], F32, kind="Internal").ap()
    ccs_d = nc.dram_tensor("ccs_d", [1, 4 * NH], F32, kind="Internal").ap()
    cco = nc.dram_tensor("cco", [4 * NH * HD, HD], F32, kind="Internal").ap()

    def sb(name, shape, dt):
        return es.enter_context(nc.sbuf_tensor(name, shape, dt))

    AR_H = sb("AR_H", [128, 32768], BF16)
    AR_X = sb("AR_X", [128, 8192], BF16)
    AR_L = sb("AR_L", [128, 8192], BF16)
    AR_W = sb("AR_W", [128, 23552], BF16)

    L_sb = sb("L_sb", [128, NH, HD], F32)
    Lbf = sb("Lbf", [128, 2, HD], BF16)
    s128n = sb("s128n", [128, MT], F32)
    shn = sb("shn", [128, MT], F32)
    mbase = sb("mbase", [128, BLK], BF16)
    diffc = sb("diffc", [128, 2, BLK], BF16)
    ddT = sb("ddT", [128, 2, 2, BLK], BF16)
    qdec = sb("qdec", [128, 2, BLK], BF16)
    kdecT = sb("kdecT", [128, NH, 2], F32)
    nvals = sb("nvals", [128, 2], F32)
    Db = sb("Db", [128, 8, NH], F32)
    srep = sb("srep", [128, NH], F32)
    snegrep = sb("snegrep", [128, NH], F32)
    wrep = sb("wrep", [128, 4, NH], F32)
    slop = sb("slop", [1, NH], F32)
    ccw = sb("ccw", [1, 4, NH], F32)
    ccm_sb = sb("ccm_sb", [1, 4], F32)
    ccp_sb = sb("ccp_sb", [1, 4], F32)
    ones_bf = sb("ones_bf", [128, 128], BF16)
    ident = sb("ident", [128, 128], BF16)
    sg_sb = sb("sg_sb", [128, HC], F32)
    su_sb = sb("su_sb", [128, HC], F32)
    sdh_sb = sb("sdh_sb", [128, DC], F32)
    nw_sb = sb("nw_sb", [128, DC], F32)
    Mt = sb("Mt", [128, 512], F32)
    Rt = sb("Rt", [128, 512], F32)
    tg_s = sb("tg_s", [128, 512], F32)
    t1_s = sb("t1_s", [128, 512], F32)
    XBG = sb("XBG", [128, DC, MT], BF16)
    SQ = sb("SQ", [128, 2, MT], BF16)
    sg_scr = sb("sg_scr", [128, 2, 512], BF16)
    qs_s = sb("qs_s", [128, 2, BLK], BF16)
    qg_s = sb("qg_s", [128, 2, 512], BF16)
    qkm_s = sb("qkm_s", [128, 2, 2, BLK], BF16)
    kT_s = sb("kT_s", [128, 2, 2, 128], BF16)
    o_s = sb("o_s", [128, 2, 512], F32)
    ga_s = sb("ga_s", [128, 2, 512], BF16)
    xd_s = sb("xd_s", [128, 2, 512], F32)
    qgl_s = sb("qgl_s", [128, 2, 512], BF16)

    M2 = tg_s
    R2 = Rt
    tu_s = t1_s
    Ssbf = AR_H[:, 28672:30720].rearrange("p (h e) -> p h e", h=NH)

    def psum(name, shape, dt):
        return es.enter_context(nc.psum_tensor(name, shape, dt))

    PS = [psum(f"PS{i}", [128, 512], F32) for i in range(7)]
    PTR = psum("PTR", [128, 512], BF16)
    PS6B = PS[6][:].bitcast(BF16)

    def S(name):
        return Sem(nc, es, name)

    pe, dv, ac, gp = S("pe"), S("dv"), S("ac"), S("gp")
    sx = [S("sx0"), S("sx1")]
    sxd = [S("sxd0"), S("sxd1")]
    sqg = [S("sqg0"), S("sqg1")]
    sol, sgl = S("sol"), S("sgl")
    swA = [S("swA0"), S("swA1"), S("swA2")]
    swB = [S("swB0"), S("swB1"), S("swB2")]
    swC = [S("swC0"), S("swC1")]
    sfA = [S("sfA0"), S("sfA1"), S("sfA2")]
    sfB = [S("sfB0"), S("sfB1"), S("sfB2")]
    sfC = [S("sfC0"), S("sfC1")]
    sbg = S("sbg")
    sto = [S("sto0"), S("sto1")]
    stq = [S("stq0"), S("stq1")]
    stg = [S("stg0"), S("stg1")]
    std = [S("std0"), S("std1")]
    scc = S("scc")
    scc2 = S("scc2")
    srw = S("srw")
    swb = S("swb")
    ccd = S("ccd")
    scst = S("scst")

    em = Em()
    em.autosync = {"vector": dv, "scalar": ac}

    H_f32 = AR_H[:].bitcast(F32)
    AXf = AR_X[:].bitcast(F32)

    def Hchunk(h):
        return AR_H[:, h * MT:(h + 1) * MT]

    def qv(h):
        return AR_H[:, h * MT:(h + 1) * MT]

    def kv_(h):
        return AR_H[:, 8192 + h * MT:8192 + (h + 1) * MT]

    def ofull(h):
        return H_f32[:, h * MT:(h + 1) * MT]

    def vT(tc):
        return AR_X[:, tc * 2048:(tc + 1) * 2048]

    def lnc(k):
        return AR_L[:, k * MT:(k + 1) * MT]

    def gal(h):
        return AR_L[:, h * MT:(h + 1) * MT]

    def wgs(s):
        return AR_W[:, s * 2048:(s + 1) * 2048].rearrange("p (c o) -> p c o", c=DC)

    def wus(s):
        return AR_W[:, 6144 + s * 2048:6144 + (s + 1) * 2048].rearrange(
            "p (c o) -> p c o", c=DC)

    def wvs(s):
        return AR_W[:, 6144 + s * 4096:6144 + (s + 1) * 4096].rearrange(
            "p (c o) -> p c o", c=DC)

    def wds(s):
        return AR_W[:, 12288 + s * 5632:12288 + (s + 1) * 5632].rearrange(
            "p (a c o) -> p a c o", a=2, c=22)

    psum_free = [0] * 8
    psum_free_sem = [dv] * 8
    sq_war = [0, 0]
    xbg_rd = {"pe": 0, "dv": 0}
    war = {"a_sg": 0, "a_m2": 0, "a_r": 0}
    freeH = []
    freeL = []
    freeX = []
    bg_jobs = []
    bg_total = [0]
    wbmarks = {}

    def bg_emit(k):
        while bg_jobs and k > 0:
            dst_ap, src_ap = bg_jobs.pop(0)
            em.op("gpsimd", "dma_start", out=dst_ap, in_=src_ap,
                  inc=sbg, incn=16)
            k -= 1
        bg_total[0] = sbg.n

    def psfree(i):
        return [(psum_free_sem[i], psum_free[i])] if psum_free[i] else []

    def set_psfree(i, sem, val):
        psum_free_sem[i] = sem
        psum_free[i] = val

    def setup():
        em.op("sync", "dma_start", out=sg_sb[:], in_=sgate, inc=scst, incn=16)
        em.op("sync", "dma_start", out=su_sb[:], in_=sup, inc=scst, incn=16)
        em.op("sync", "dma_start", out=sdh_sb[:], in_=sdown, inc=scst, incn=16)
        em.op("sync", "dma_start", out=nw_sb[:], in_=nwin, inc=scst, incn=16)
        em.op("sync", "dma_start", out=slop[:], in_=slopes, inc=scst, incn=16)
        em.op("sync", "dma_start", out=ccm_sb[:], in_=ccm, inc=scst, incn=16)
        em.op("sync", "dma_start", out=ccp_sb[:], in_=ccp, inc=scst, incn=16)
        ldall = (scst, scst.n)

        em.op("vector", "memset", ones_bf[:], 1.0, inc=dv)
        em.op("vector", "memset", L_sb[:].rearrange("p h e -> p (h e)"), 0.0,
              inc=dv)
        v_ones = dv.n
        em.op("vector", "tensor_scalar_mul", sdh_sb[:], sdh_sb[:], 0.5,
              waits=[ldall], inc=dv)
        em.op("gpsimd", "affine_select", ident[:], ones_bf[:],
              pattern=[[1, 128]], compare_op=ALU.is_equal, fill=0.0, base=0,
              channel_multiplier=-1, waits=[(dv, v_ones)], inc=gp)
        em.op("gpsimd", "iota", mbase[:], pattern=[[1, BLK]], base=1,
              channel_multiplier=0, allow_small_or_imprecise_dtypes=True,
              inc=gp)
        em.op("gpsimd", "iota", diffc[:], pattern=[[-128, 2], [1, BLK]],
              base=0, channel_multiplier=-1,
              allow_small_or_imprecise_dtypes=True, inc=gp)
        em.op("gpsimd", "iota", nvals[:], pattern=[[128, 2]], base=-255,
              channel_multiplier=1, allow_small_or_imprecise_dtypes=True,
              inc=gp)
        g_iota = gp.n
        em.op("vector", "tensor_single_scalar", ddT[:, 0], diffc[:], 0.0,
              ALU.is_lt, waits=[(gp, g_iota)], inc=dv)
        em.op("vector", "tensor_scalar_max", diffc[:], diffc[:], 0.0, inc=dv)
        em.op("vector", "scalar_tensor_tensor", diffc[:], ddT[:, 0], 1.0e6,
              diffc[:], ALU.mult, ALU.add, inc=dv)
        em.op("sync", "dma_start", out=srep[:],
              in_=slopes.broadcast_to([128, NH]), waits=[ldall],
              inc=scst, incn=16)
        g_rep = 0
        rep_ld = (scst, scst.n)
        em.op("vector", "tensor_scalar_mul", snegrep[:], srep[:], -1.0,
              waits=[rep_ld], inc=dv)
        v_srep = dv.n
        for b in range(8):
            em.op("scalar", "activation", Db[:, b, :], srep[:], AF.Exp,
                  scale=float(-256.0 * b),
                  waits=[rep_ld, (dv, v_srep)], inc=ac)
        for h in range(NH):
            em.op("scalar", "activation", kdecT[:, h, :], nvals[:], AF.Exp,
                  scale=srep[:, h:h + 1],
                  waits=[(dv, v_srep), (gp, g_iota), rep_ld], inc=ac)
        for j in range(4):
            em.op("scalar", "activation", ccw[:, j, :], slop[:], AF.Exp,
                  scale=ccp_sb[:, j:j + 1], waits=[ldall], inc=ac)
        a_ccw = ac.n
        for j in range(4):
            em.op("vector", "tensor_scalar_mul", ccw[:, j, :], ccw[:, j, :],
                  ccm_sb[:, j:j + 1], waits=[(ac, a_ccw)], inc=dv)
        v_ccw = dv.n
        em.op("gpsimd", "dma_start", out=ccs_d,
              in_=ccw[:].rearrange("p j h -> p (j h)"),
              waits=[(dv, v_ccw)], inc=srw, incn=16)
        em.op("gpsimd", "dma_start",
              out=wrep[:].rearrange("p j h -> p (j h)"),
              in_=ccs_d.broadcast_to([128, 4 * NH]),
              waits=[(srw, srw.n)], inc=srw, incn=16)
        setup.marks = {"dv": dv.n, "ac": ac.n, "gp": gp.n,
                       "wrep": (srw, srw.n)}

    def stats_chunk(k, xsrc_ap, ld_wait, sq_extra=()):
        sl = k % 2
        wq = list(ld_wait) + list(sq_extra)
        if sq_war[sl]:
            wq.append((pe, sq_war[sl]))
        em.op("scalar", "activation", SQ[:, sl, :], xsrc_ap, AF.Square,
              waits=wq, inc=ac)
        va = ac.n
        wts = list(ld_wait) + (psfree(4) if k == 0 else [])
        em.op("tensor", "matmul", PS[4][:], ones_bf[:], xsrc_ap,
              start=(k == 0), stop=(k == DC - 1), waits=wts, inc=pe)
        wts = [(ac, va)] + (psfree(5) if k == 0 else [])
        em.op("tensor", "matmul", PS[5][:], ones_bf[:], SQ[:, sl, :],
              start=(k == 0), stop=(k == DC - 1), waits=wts, inc=pe)
        sq_war[sl] = pe.n

    def layernorm(mt, src, want_s, pre=None):
        em.phase = f"LN(pre={pre is not None})"
        """LN over features, input resident in XBG (bf16). If pre is None,
        fill XBG from HBM src and accumulate stats; else pre is the pe-mark
        of already-accumulated stats in PS4 (sum) / PS5 (sumsq)."""
        t0 = mt * MT
        if pre is None:
            for k in range(DC):
                sl = k % 2
                wxd = [(std[sl], std[sl].n)]
                if xd_free[sl][0]:
                    wxd += [(dv, xd_free[sl][0]), (ac, xd_free[sl][1])]
                em.op("sync", "dma_start", out=xd_s[:, sl, :],
                      in_=src[k * 128:(k + 1) * 128, t0:t0 + MT],
                      waits=wxd, inc=sxd[sl], incn=16)
                x_ld = sxd[sl].n
                w0 = [(sxd[sl], x_ld)]
                if k == 0:
                    w0.append((pe, xbg_rd["pe"]))
                em.op("vector", "tensor_copy", XBG[:, k, :], xd_s[:, sl, :],
                      waits=w0, inc=dv)
                vb = dv.n
                wq = [(sxd[sl], x_ld)]
                if sq_war[sl]:
                    wq.append((pe, sq_war[sl]))
                em.op("scalar", "activation", SQ[:, sl, :], xd_s[:, sl, :],
                      AF.Square, waits=wq, inc=ac)
                va = ac.n
                wts = [(dv, vb)] + (psfree(4) if k == 0 else [])
                em.op("tensor", "matmul", PS[4][:], ones_bf[:], XBG[:, k, :],
                      start=(k == 0), stop=(k == DC - 1), waits=wts, inc=pe)
                wts = [(ac, va)] + (psfree(5) if k == 0 else [])
                em.op("tensor", "matmul", PS[5][:], ones_bf[:], SQ[:, sl, :],
                      start=(k == 0), stop=(k == DC - 1), waits=wts, inc=pe)
                sq_war[sl] = pe.n
                xd_free[sl] = (vb, va)
            p_stats = pe.n
        else:
            p_stats = pre
        em.op("vector", "tensor_scalar_mul", Mt[:], PS[4][:], 1.0 / D,
              waits=[(pe, p_stats), (ac, war["a_m2"])], inc=dv)
        set_psfree(4, dv, dv.n)
        v_m = dv.n
        em.op("scalar", "activation", M2[:], Mt[:], AF.Square,
              waits=[(dv, v_m)], inc=ac)
        a_m2 = ac.n
        war["a_m2"] = a_m2
        em.op("vector", "scalar_tensor_tensor", Rt[:], PS[5][:], 1.0 / D,
              M2[:], ALU.mult, ALU.subtract,
              waits=[(ac, a_m2), (ac, war["a_r"])], inc=dv)
        set_psfree(5, dv, dv.n)
        em.op("vector", "tensor_scalar_add", Rt[:], Rt[:], EPS_LN, inc=dv)
        em.op("vector", "reciprocal", Rt[:], Rt[:], inc=dv)
        v_v = dv.n
        em.op("scalar", "activation", Rt[:], Rt[:], AF.Sqrt,
              waits=[(dv, v_v)], inc=ac)
        a_r = ac.n
        war["a_r"] = a_r
        first_write = True
        nmarks = []
        ps_started = [False]
        for cg in range(4):
            Mbc = Mt[:].unsqueeze(1).broadcast_to([128, 4, 512])
            Rbc = Rt[:].unsqueeze(1).broadcast_to([128, 4, 512])
            dstv = AR_L[:, cg * 2048:(cg + 1) * 2048].rearrange(
                "p (c t) -> p c t", c=4)
            wl = [(pe, p_stats)] if cg == 0 else []
            if first_write:
                wl += list(freeL)
                freeL.clear()
                first_write = False
            em.op("vector", "tensor_tensor", dstv,
                  XBG[:, cg * 4:(cg + 1) * 4, :], Mbc, ALU.subtract,
                  waits=wl, inc=dv)
            em.op("vector", "tensor_tensor", dstv, dstv, Rbc, ALU.mult,
                  waits=([(ac, a_r)] if cg == 0 else []), inc=dv)
            nmarks.append(dv.n)
            if want_s:
                for j in range(4):
                    k = cg * 4 + j
                    wts = [(dv, nmarks[-1])] if j == 0 else []
                    if k == 0:
                        wts = wts + psfree(4)
                    em.op("tensor", "matmul", PS[4][:], ones_bf[:], lnc(k),
                          start=(k == 0), stop=(k == DC - 1), waits=wts,
                          inc=pe)
        s_mark = 0
        if want_s:
            p_s = pe.n
            em.op("vector", "tensor_scalar_mul", s128n[:], PS[4][:], -128.0,
                  waits=[(pe, p_s)], inc=dv)
            set_psfree(4, dv, dv.n)
            s_mark = dv.n
        xbg_rd["pe"] = p_stats
        xbg_rd["dv"] = nmarks[-1]
        return nmarks, s_mark

    class WStream:
        def __init__(self, sems_cast, sems_bf, slot_fn, init_free, nslots=2,
                     gate=None):
            self.sc, self.sf = sems_cast, sems_bf
            self.slot = slot_fn
            self.ns = nslots
            self.freeat = [init_free] * nslots
            self.loaded = [None] * nslots
            self.wb = [(swb, swb.n) if swb.n else None] * nslots
            self.gate = gate

        def issue(self, i, src_ap, cached, dst=None, bg=0, wb_dst=None):
            s = i % self.ns
            w = [self.freeat[s]] if self.freeat[s] is not None else []
            if self.wb[s] is not None:
                w.append(self.wb[s])
                self.wb[s] = None
            tgt = dst if dst is not None else self.slot(s)[:]
            if cached:
                w.append(self.gate if self.gate else (sbg, bg_total[0]))
                em.op("scalar", "dma_start", out=tgt, in_=src_ap,
                      waits=w, inc=self.sf[s], incn=16)
                self.loaded[s] = (self.sf[s], self.sf[s].n)
            else:
                em.op("gpsimd", "dma_start", out=tgt, in_=src_ap,
                      waits=w, inc=self.sc[s], incn=16)
                self.loaded[s] = (self.sc[s], self.sc[s].n)
                if wb_dst is not None:
                    em.op("sync", "dma_start", out=wb_dst, in_=tgt,
                          waits=[self.loaded[s]], inc=swb, incn=16)
                    self.wb[s] = (swb, swb.n)
            if bg:
                bg_emit(bg)

        def consumed(self, i, mark):
            self.freeat[i % self.ns] = mark

    def ffn(mt, call, dst):
        t0 = mt * MT
        sgH_mark = ffn.sgH

        cached = (call == 1) or mt >= 2
        pmark = (pe, pe.n)
        stG = WStream(swA, sfA, wgs, pmark, nslots=3)
        stU = WStream(swB, sfB, wus, pmark, nslots=3)
        stD = WStream(swC, sfC, wds, pmark)

        def gsrc(h):
            return (wgb if cached else wg)[h].rearrange("p c o -> p c o")

        def usrc(h):
            return (wub if cached else wu)[h].rearrange("p c o -> p c o")

        def dsrc(d):
            return (wdb if cached else wd)[d].rearrange(
                "d2 p c o -> p d2 c o")

        for i0 in range(3):
            stG.issue(i0, gsrc(i0), cached)
            stU.issue(i0, usrc(i0), cached)
        nmarks, s_mark = layernorm(mt, xin if call == 0 else x2h, True,
                                   stats_pre[0])
        stats_pre[0] = None
        ln_done = nmarks[3]
        em.phase = f"ffn{call}-gateup"
        h_last = 0
        first_H = True
        sh_q = []
        for h in range(HC):
            s = h % 3
            gi, ui = h % 2, 2 + h % 2
            wts = [stG.loaded[s]] + psfree(gi)
            if h > 0:
                wts.append((dv, ln_done))
            for k in range(DC):
                kw = wts if k == 0 else []
                if h == 0:
                    kw = list(kw) + [(dv, nmarks[k // 4])]
                em.op("tensor", "matmul", PS[gi][:], wgs(s)[:, k, :], lnc(k),
                      start=(k == 0), stop=(k == DC - 1),
                      waits=kw,
                      inc=(pe if k == DC - 1 else None))
            stG.consumed(h, (pe, pe.n))
            p_g = pe.n
            wts = [stU.loaded[s]] + psfree(ui)
            for k in range(DC):
                kw = wts if k == 0 else []
                if h == 0:
                    kw = list(kw) + [(dv, nmarks[k // 4])]
                em.op("tensor", "matmul", PS[ui][:], wus(s)[:, k, :], lnc(k),
                      start=(k == 0), stop=(k == DC - 1),
                      waits=kw,
                      inc=(pe if k == DC - 1 else None))
            stU.consumed(h, (pe, pe.n))
            p_u = pe.n
            if h + 3 < HC:
                stG.issue(h + 3, gsrc(h + 3), cached)
                stU.issue(h + 3, usrc(h + 3), cached)
            if h == 36:
                stD.issue(0, dsrc(0), cached)
                stD.issue(1, dsrc(1), cached)
            em.op("vector", "tensor_tensor", tg_s[:], PS[gi][:], s128n[:],
                  ALU.add,
                  waits=[(pe, p_g), (dv, s_mark), (ac, war["a_sg"])], inc=dv)
            set_psfree(gi, dv, dv.n)
            v_tg = dv.n
            s2 = h % 2
            wsg = [(dv, v_tg)]
            if sgH_mark[s2]:
                wsg.append((dv, sgH_mark[s2]))
            em.op("scalar", "activation", sg_scr[:, s2, :], tg_s[:], AF.Silu,
                  scale=sg_sb[:, h:h + 1], waits=wsg, inc=ac)
            a_sg = ac.n
            war["a_sg"] = a_sg
            em.op("vector", "tensor_tensor", tu_s[:], PS[ui][:], s128n[:],
                  ALU.add, waits=[(pe, p_u)], inc=dv)
            set_psfree(ui, dv, dv.n)
            em.wait("vector", dv, dv.n)
            wl = list(freeH) if first_H else []
            first_H = False
            freeH.clear()
            em.op("vector", "scalar_tensor_tensor", Hchunk(h), tu_s[:],
                  su_sb[:, h:h + 1], sg_scr[:, s2, :], ALU.mult, ALU.mult,
                  waits=[(ac, a_sg)] + wl, inc=dv)
            h_last = dv.n
            sgH_mark[s2] = h_last
            sh_q.append((h, h_last))
            while len(sh_q) > 6:
                hh, hm = sh_q.pop(0)
                wts = [(dv, hm)] + (psfree(6) if hh == 0 else [])
                em.op("tensor", "matmul", PS[6][:], ones_bf[:], Hchunk(hh),
                      start=(hh == 0), stop=(hh == HC - 1), waits=wts, inc=pe)
        while sh_q:
            hh, hm = sh_q.pop(0)
            wts = [(dv, hm)] + (psfree(6) if hh == 0 else [])
            em.op("tensor", "matmul", PS[6][:], ones_bf[:], Hchunk(hh),
                  start=(hh == 0), stop=(hh == HC - 1), waits=wts, inc=pe)
        p_sh = pe.n
        em.op("vector", "tensor_scalar_mul", shn[:], PS[6][:], -128.0,
              waits=[(pe, p_sh)], inc=dv)
        set_psfree(6, dv, dv.n)
        v_shn = dv.n
        em.phase = f"ffn{call}-down"
        xsrc = xin if call == 0 else x2h
        pre_mark = None
        stats_q = []
        for d in range(DC):
            sl = d % 2
            wxd = [(std[sl], std[sl].n)]
            if xd_free[sl][0]:
                wxd += [(dv, xd_free[sl][0]), (ac, xd_free[sl][1])]
            em.op("sync", "dma_start", out=xd_s[:, sl, :],
                  in_=xsrc[d * 128:(d + 1) * 128, t0:t0 + MT],
                  waits=wxd, inc=sxd[sl], incn=16)
            x_ld = sxd[sl].n
            pi = d % 4
            for j in range(HC):
                wts = []
                if j == 0:
                    wts = psfree(pi) + [stD.loaded[d % 2], (dv, h_last)]
                em.op("tensor", "matmul", PS[pi][:],
                      wds(d % 2)[:, j // 22, j % 22, :],
                      Hchunk(j), start=(j == 0), stop=(j == HC - 1),
                      waits=wts, inc=(pe if j == HC - 1 else None))
            p_d = pe.n
            while len(stats_q) > 1:
                dd, ssl, vb, va = stats_q.pop(0)
                wts = [(dv, vb)] + (psfree(4) if dd == 0 else [])
                em.op("tensor", "matmul", PS[4][:], ones_bf[:],
                      XBG[:, dd, :], start=(dd == 0), stop=(dd == DC - 1),
                      waits=wts, inc=pe)
                wts = [(ac, va)] + (psfree(5) if dd == 0 else [])
                em.op("tensor", "matmul", PS[5][:], ones_bf[:],
                      SQ[:, ssl, :], start=(dd == 0), stop=(dd == DC - 1),
                      waits=wts, inc=pe)
                sq_war[ssl] = pe.n
            if d + 2 < DC:
                stD.consumed(d, (pe, p_d))
                stD.issue(d + 2, dsrc(d + 2), cached)
            em.op("vector", "tensor_tensor", t1_s[:], PS[pi][:], shn[:],
                  ALU.add, waits=[(pe, p_d), (dv, v_shn)], inc=dv)
            set_psfree(pi, dv, dv.n)
            em.op("vector", "scalar_tensor_tensor", xd_s[:, sl, :], t1_s[:],
                  sdh_sb[:, d:d + 1], xd_s[:, sl, :], ALU.mult, ALU.add,
                  waits=[(sxd[sl], x_ld)], inc=dv)
            v_x1 = dv.n
            em.op("sync", "dma_start",
                  out=dst[d * 128:(d + 1) * 128, t0:t0 + MT],
                  in_=xd_s[:, sl, :], waits=[(dv, v_x1)], inc=std[sl], incn=16)
            if call == 0:
                wpc = [(pe, sq_war[sl])] if sq_war[sl] else []
                if d == 0:
                    wpc.append((pe, xbg_rd["pe"]))
                em.op("vector", "tensor_copy", XBG[:, d, :],
                      xd_s[:, sl, :], waits=wpc, inc=dv)
                vb = dv.n
                em.op("scalar", "activation", SQ[:, sl, :],
                      xd_s[:, sl, :], AF.Square,
                      waits=[(dv, v_x1)] + wpc, inc=ac)
                va = ac.n
                stats_q.append((d, sl, vb, va))
                xd_free[sl] = (vb, va)
        while stats_q:
            dd, ssl, vb, va = stats_q.pop(0)
            wts = [(dv, vb)] + (psfree(4) if dd == 0 else [])
            em.op("tensor", "matmul", PS[4][:], ones_bf[:],
                  XBG[:, dd, :], start=(dd == 0), stop=(dd == DC - 1),
                  waits=wts, inc=pe)
            wts = [(ac, va)] + (psfree(5) if dd == 0 else [])
            em.op("tensor", "matmul", PS[5][:], ones_bf[:],
                  SQ[:, ssl, :], start=(dd == 0), stop=(dd == DC - 1),
                  waits=wts, inc=pe)
            sq_war[ssl] = pe.n
        if call == 0:
            pre_mark = pe.n
        freeH.append((pe, pe.n))
        freeL.append((pe, pe.n))
        return pre_mark

    ffn.sgH = [0, 0]
    stats_pre = [None]
    xd_free = [(0, 0), (0, 0)]

    scan = {"o_ctr": 0, "qk_ctr": 0, "dd_free": [0, 0], "qs_free": [0, 0],
            "qkm_free": [0, 0], "kT_free": [0, 0], "tr_free": [0, 0],
            "o_free": [0, 0], "oS_free": [0, 0], "Lb_free": [0, 0],
            "Lb_mark": [0, 0]}
    ga_free = [0, 0]
    qg_st_free = [0, 0]

    def attn1(mt, pre):
        attn1.lmark = getattr(attn1, "lmark", [0] * NH)
        t0 = mt * MT
        cached = mt >= 1
        agate = (swb, wbmarks["attn"]) if cached else None
        pmark = (pe, pe.n)
        nmarks, _ = layernorm(mt, x1h, False, pre)
        ln_done = nmarks[3]
        stQ = WStream(swA, sfA, wgs, pmark, nslots=3, gate=agate)
        qk_list = []
        for h in range(NH):
            qk_list.append((3 * h, True, h))
            qk_list.append((3 * h + 1, False, h))

        def qsrc(o):
            return (wqkvb if cached else wqkv)[o].rearrange("p c o -> p c o")

        def qwb(o):
            return None if cached else wqkvb[o].rearrange("p c o -> p c o")

        for i0 in range(3):
            stQ.issue(i0, qsrc(qk_list[i0][0]), cached,
                      wb_dst=qwb(qk_list[i0][0]))
        em.phase = "qkproj"
        firstH = True
        for i, (o, is_q, h) in enumerate(qk_list):
            s = i % 3
            pi = i % 4
            wts = [stQ.loaded[s]] + psfree(pi)
            if i > 0:
                wts.append((dv, ln_done))
            for k in range(DC):
                kw = wts if k == 0 else []
                if i == 0:
                    kw = list(kw) + [(dv, nmarks[k // 4])]
                em.op("tensor", "matmul", PS[pi][:], wgs(s)[:, k, :], lnc(k),
                      start=(k == 0), stop=(k == DC - 1),
                      waits=kw,
                      inc=(pe if k == DC - 1 else None))
            stQ.consumed(i, (pe, pe.n))
            p_mm = pe.n
            if i + 3 < len(qk_list):
                stQ.issue(i + 3, qsrc(qk_list[i + 3][0]), cached, bg=2,
                          wb_dst=qwb(qk_list[i + 3][0]))
            wts = [(pe, p_mm)]
            if firstH:
                wts += freeH
                freeH.clear()
                firstH = False
            em.op("scalar", "activation", qv(h) if is_q else kv_(h),
                  PS[pi][:], AF.Silu, waits=wts, inc=ac)
            set_psfree(pi, ac, ac.n)
        vload = [None, None]
        vfree = [pmark, pmark]
        v_wb = [(swb, swb.n) if swb.n else None] * 2

        def v_issue(g):
            s = g % 2
            w = [vfree[s]] if vfree[s] else []
            if v_wb[s] is not None:
                w.append(v_wb[s])
                v_wb[s] = None
            if cached:
                w.append(agate)
                em.op("scalar", "dma_start", out=wvs(s)[:, :, :],
                      in_=wvb[g], waits=w, inc=sfB[s], incn=16)
                vload[s] = (sfB[s], sfB[s].n)
            else:
                em.op("gpsimd", "dma_start", out=wvs(s)[:, :, :],
                      in_=wvin[g], waits=w, inc=swB[s], incn=16)
                vload[s] = (swB[s], swB[s].n)
                em.op("sync", "dma_start", out=wvb[g],
                      in_=wvs(s)[:, :, :], waits=[vload[s]],
                      inc=swb, incn=16)
                v_wb[s] = (swb, swb.n)
            bg_emit(2)

        em.phase = "vproj"
        v_issue(0)
        v_issue(1)
        firstX = True
        for g in range(NH // 2):
            s = g % 2
            for tc in range(TC):
                pi = (g * TC + tc) % 4
                wts = [vload[s]] + psfree(pi)
                for k in range(DC):
                    em.op("tensor", "matmul", PS[pi][:, 0:256],
                          lnc(k)[:, tc * 128:(tc + 1) * 128], wvs(s)[:, k, :],
                          start=(k == 0), stop=(k == DC - 1),
                          waits=wts if k == 0 else (),
                          inc=(pe if k == DC - 1 else None))
                p_v = pe.n
                wts2 = [(pe, p_v)]
                if firstX:
                    wts2 += freeX
                    freeX.clear()
                    firstX = False
                em.op("scalar", "activation",
                      vT(tc)[:, g * 256:(g + 1) * 256], PS[pi][:, 0:256],
                      AF.Silu, waits=wts2, inc=ac)
                set_psfree(pi, ac, ac.n)
            vfree[s] = (pe, pe.n)
            if g + 2 < NH // 2:
                v_issue(g + 2)
        def gasrc(o):
            return (wgab if cached else wga)[o]

        def gawb(o):
            return None if cached else wgab[o]

        em.phase = "gateproj"
        nq = len(qk_list)
        stQ.issue(nq, gasrc(0), cached, wb_dst=gawb(0))
        stQ.issue(nq + 1, gasrc(1), cached, wb_dst=gawb(1))
        stQ.issue(nq + 2, gasrc(2), cached, wb_dst=gawb(2))
        for o in range(DC):
            s = (nq + o) % 3
            pi = o % 4
            wts = [stQ.loaded[s], (dv, ln_done)] + psfree(pi)
            for k in range(DC):
                em.op("tensor", "matmul", PS[pi][:], wgs(s)[:, k, :], lnc(k),
                      start=(k == 0), stop=(k == DC - 1),
                      waits=wts if k == 0 else (),
                      inc=(pe if k == DC - 1 else None))
            stQ.consumed(nq + o, (pe, pe.n))
            p_mm = pe.n
            if o + 3 < DC:
                stQ.issue(nq + o + 3, gasrc(o + 3), cached, bg=2,
                          wb_dst=gawb(o + 3))
            gs = o % 2
            w2 = [(pe, p_mm)]
            if ga_free[gs]:
                w2.append((stg[gs], ga_free[gs]))
            em.op("scalar", "activation", ga_s[:, gs, :], PS[pi][:],
                  AF.Sigmoid, waits=w2, inc=ac)
            set_psfree(pi, ac, ac.n)
            a_g = ac.n
            em.op("sync", "dma_start",
                  out=gah[o * 128:(o + 1) * 128, t0:t0 + MT],
                  in_=ga_s[:, gs, :], waits=[(ac, a_g)], inc=stg[gs], incn=16)
            ga_free[gs] = stg[gs].n
        a_evac = ac.n
        em.wait("tensor", ac, a_evac)
        em.phase = "scan"
        lmark = attn1.lmark

        def build_dd(h):
            hs = h % 2
            wts = [(dv, setup.marks["dv"]), (ac, setup.marks["ac"]),
                   (dv, dv.n)]
            em.op("scalar", "activation",
                  ddT[:, hs].rearrange("p a b -> p (a b)"),
                  diffc[:].rearrange("p a b -> p (a b)"), AF.Exp,
                  scale=snegrep[:, h:h + 1], waits=wts, inc=ac)
            v_dd = ac.n
            em.op("scalar", "activation", qdec[:, hs, :], mbase[:], AF.Exp,
                  scale=snegrep[:, h:h + 1], inc=ac)
            return v_dd, ac.n

        def stage1(h, b, v_dd, a_qd):
            hs = h % 2
            blkg = mt * NB + b
            mcols = slice(b * BLK, (b + 1) * BLK)
            sc = scan["o_ctr"]
            scan["o_ctr"] += 1
            if blkg > 0 and b == 0:
                wts = []
                if scan["Lb_free"][hs]:
                    wts.append((pe, scan["Lb_free"][hs]))
                em.op("vector", "tensor_copy", Lbf[:, hs, :], L_sb[:, h, :],
                      waits=wts, inc=dv)
                scan["Lb_mark"][hs] = dv.n
            wts = [(ac, a_qd), (ac, a_evac)]
            if scan["qs_free"][hs]:
                wts.append((pe, scan["qs_free"][hs]))
            em.op("vector", "tensor_mul", qs_s[:, hs, :], qv(h)[:, mcols],
                  qdec[:, hs, :], waits=wts, inc=dv)
            v_qs = dv.n
            em.wait("vector", dv, v_qs)
            qgs = h % 2
            wts = [(stq[qgs], qg_st_free[qgs])] if (qg_st_free[qgs] and b == 0) else []
            em.op("vector", "tensor_scalar_mul",
                  qg_s[:, qgs, b * BLK:(b + 1) * BLK], qs_s[:, hs, :],
                  Db[:, blkg, h:h + 1], waits=wts, inc=dv)
            v_qg = dv.n
            if b == NB - 1:
                em.op("sync", "dma_start",
                      out=qgh[h * 128:(h + 1) * 128, t0:t0 + MT],
                      in_=qg_s[:, qgs, :], waits=[(dv, v_qg)],
                      inc=stq[qgs], incn=16)
                qg_st_free[qgs] = stq[qgs].n
            qkpi = sc % 2
            wts = psfree(qkpi)
            for ncx in range(2):
                em.op("tensor", "matmul",
                      PS[qkpi][:, ncx * 256:(ncx + 1) * 256],
                      kv_(h)[:, b * BLK + ncx * 128:b * BLK + (ncx + 1) * 128],
                      qv(h)[:, mcols], start=True, stop=True,
                      waits=wts if ncx == 0 else (), inc=pe)
            p_qk = pe.n
            trp = sc % 2
            trdst = PTR if trp == 0 else PS6B
            wts = [(gp, setup.marks["gp"])]
            if scan["tr_free"][trp]:
                wts.append((ac, scan["tr_free"][trp]))
            if trp == 1 and psum_free[6]:
                wts.append((psum_free_sem[6], psum_free[6]))
                psum_free[6] = 0
            for ncx in range(2):
                em.op("tensor", "transpose",
                      trdst[:, ncx * 128:(ncx + 1) * 128],
                      kv_(h)[:, b * BLK + ncx * 128:b * BLK + (ncx + 1) * 128],
                      ident[:], waits=wts if ncx == 0 else (), inc=pe)
            p_tr = pe.n
            cpi = 2 + sc % 2
            n_o = 0
            if blkg > 0:
                wts = [(dv, scan["Lb_mark"][hs]), (dv, v_qs)] + psfree(cpi)
                em.op("tensor", "matmul", PS[cpi][:, 256:512],
                      Lbf[:, hs, :], qs_s[:, hs, :], start=True, stop=False,
                      waits=wts, inc=pe)
                n_o = 1
                scan["Lb_free"][hs] = pe.n
            scan["qs_free"][hs] = pe.n
            qkms = sc % 2
            wts = [(pe, p_qk), (ac, v_dd)]
            if scan["qkm_free"][qkms]:
                wts.append((pe, scan["qkm_free"][qkms]))
            em.op("vector", "tensor_tensor",
                  qkm_s[:, qkms, :, :].rearrange("p a b -> p (a b)"),
                  PS[qkpi][:],
                  ddT[:, hs].rearrange("p a b -> p (a b)"),
                  ALU.mult, waits=wts, inc=dv)
            v_qkm = dv.n
            set_psfree(qkpi, dv, v_qkm)
            wts = [(pe, p_tr)]
            if scan["kT_free"][trp]:
                wts.append((pe, scan["kT_free"][trp]))
            for ncx in range(2):
                em.op("scalar", "activation", kT_s[:, trp, ncx, :],
                      trdst[:, ncx * 128:(ncx + 1) * 128],
                      AF.Identity, scale=kdecT[:, h, ncx:ncx + 1],
                      waits=wts if ncx == 0 else (), inc=ac)
            a_kT = ac.n
            scan["tr_free"][trp] = a_kT
            return dict(sc=sc, v_qkm=v_qkm, a_kT=a_kT, n_o=n_o, h=h, b=b)

        def stage2(st):
            h, b = st["h"], st["b"]
            hs = h % 2
            sc = st["sc"]
            qkms = sc % 2
            trp = sc % 2
            cpi = 2 + sc % 2
            trdst = PTR if trp == 0 else PS6B
            oreg = PS[cpi][:, 256:512]
            for ncx in range(2):
                tc = b * 2 + ncx
                wts = []
                if ncx == 0:
                    wts = [(dv, st["v_qkm"])]
                    if st["n_o"] == 0:
                        wts += psfree(cpi)
                em.op("tensor", "matmul", oreg,
                      vT(tc)[:, h * 128:(h + 1) * 128],
                      qkm_s[:, qkms, ncx, :],
                      start=(st["n_o"] == 0 and ncx == 0), stop=(ncx == 1),
                      waits=wts, inc=pe)
            p_o = pe.n
            scan["qkm_free"][qkms] = p_o
            wts = [(ac, st["a_kT"])]
            for ncx in range(2):
                tc = b * 2 + ncx
                em.op("tensor", "matmul", PS[cpi][:, 0:128],
                      kT_s[:, trp, ncx, :],
                      vT(tc)[:, h * 128:(h + 1) * 128],
                      start=(ncx == 0), stop=(ncx == 1),
                      waits=wts if ncx == 0 else (), inc=pe)
            p_c = pe.n
            scan["kT_free"][trp] = p_c
            em.op("vector", "scalar_tensor_tensor", L_sb[:, h, :],
                  L_sb[:, h, :], Db[:, 1, h:h + 1], PS[cpi][:, 0:128],
                  ALU.mult, ALU.add, waits=[(pe, p_c)], inc=dv)
            set_psfree(cpi, dv, dv.n)
            lmark[h] = dv.n
            em.wait("vector", dv, dv.n)
            if b + 1 < NB:
                wts = []
                if scan["Lb_free"][hs]:
                    wts.append((pe, scan["Lb_free"][hs]))
                em.op("vector", "tensor_copy", Lbf[:, hs, :], L_sb[:, h, :],
                      waits=wts, inc=dv)
                scan["Lb_mark"][hs] = dv.n
            osl = h % 2
            wts = [(pe, p_o)]
            if scan["oS_free"][osl] and b == 0:
                wts.append((sto[osl], scan["oS_free"][osl]))
            em.op("vector", "tensor_copy",
                  o_s[:, osl, b * BLK:(b + 1) * BLK], oreg,
                  waits=wts, inc=dv)
            set_psfree(cpi, dv, dv.n)
            v_oe = dv.n
            if b == NB - 1:
                em.op("sync", "dma_start",
                      out=oh[h * 128:(h + 1) * 128, t0:t0 + MT],
                      in_=o_s[:, osl, :], waits=[(dv, v_oe)],
                      inc=sto[osl], incn=16)
                scan["oS_free"][osl] = sto[osl].n

        pf_mark = [0]

        def ln1_fill_all():
            t1n = (mt + 1) * MT
            for k in range(DC):
                w = []
                if k == 0:
                    w = [(pe, xbg_rd["pe"]), (dv, xbg_rd["dv"])]
                em.op("gpsimd", "dma_start", out=XBG[:, k, :],
                      in_=xin[k * 128:(k + 1) * 128, t1n:t1n + MT],
                      waits=w, inc=sx[0], incn=16)
            pf_mark[0] = sx[0].n

        def ln1_stats(k):
            sl = k % 2
            wq = [(sx[0], pf_mark[0])]
            if sq_war[sl]:
                wq.append((pe, sq_war[sl]))
            em.op("scalar", "activation", SQ[:, sl, :], XBG[:, k, :],
                  AF.Square, waits=wq, inc=ac)
            va = ac.n
            wts = [(sx[0], pf_mark[0])] + (psfree(4) if k == 0 else [])
            em.op("tensor", "matmul", PS[4][:], ones_bf[:], XBG[:, k, :],
                  start=(k == 0), stop=(k == DC - 1), waits=wts, inc=pe)
            wts = [(ac, va)] + (psfree(5) if k == 0 else [])
            em.op("tensor", "matmul", PS[5][:], ones_bf[:], SQ[:, sl, :],
                  start=(k == 0), stop=(k == DC - 1), waits=wts, inc=pe)
            sq_war[sl] = pe.n

        if mt < NMT - 1:
            ln1_fill_all()
        for g in range(NH // 2):
            hA, hB = 2 * g, 2 * g + 1
            ddA = build_dd(hA)
            ddB = build_dd(hB)
            for b in range(NB):
                sA = stage1(hA, b, *ddA)
                sB = stage1(hB, b, *ddB)
                stage2(sA)
                stage2(sB)
            if mt < NMT - 1 and g >= 4:
                ln1_stats(2 * (g - 4))
                ln1_stats(2 * (g - 4) + 1)
        if mt < NMT - 1:
            for k in range(8, DC):
                ln1_stats(k)
        scan["dd_free"][0] = dv.n
        scan["dd_free"][1] = dv.n
        if scan["tr_free"][1]:
            set_psfree(6, ac, scan["tr_free"][1])
        if mt < NMT - 1:
            stats_pre[0] = pe.n
        freeH.append((pe, pe.n))
        freeH.append((dv, dv.n))
        freeL.append((pe, pe.n))
        freeX.append((pe, pe.n))
        set_psfree(6, dv, max(scan["o_free"]))
        if mt == 0:
            wbmarks["attn"] = swb.n
        if mt == 1:
            bg_emit(10 ** 9)

    def cc_phase():
        v_all = dv.n
        p_all = pe.n
        bg_emit(10 ** 9)
        em.op("gpsimd", "dma_start",
              out=cci.rearrange("(h d) e -> d h e", d=128),
              in_=L_sb[:],
              waits=[(dv, v_all)], inc=scc, incn=16)
        em.wait("gpsimd", scc, scc.n)
        em.op("gpsimd", "collective_compute", "AllGather", ALU.bypass,
              replica_groups=[[0, 1, 2, 3], [4, 5, 6, 7]],
              ins=[cci], outs=[cco], inc=ccd)
        wH = list(freeH)
        freeH.clear()
        pf = {}
        em.wait("sync", sto[0], sto[0].n)
        em.wait("sync", sto[1], sto[1].n)
        em.wait("sync", stq[0], stq[0].n)
        em.wait("sync", stq[1], stq[1].n)
        em.wait("sync", stg[0], stg[0].n)
        em.wait("sync", stg[1], stg[1].n)
        em.op("sync", "dma_start",
              out=H_f32[:, 0:8192].rearrange("p (h t) -> p h t", h=NH),
              in_=oh[:, 0:MT].rearrange("(h p) t -> p h t", p=128),
              waits=wH, inc=sol, incn=16)
        pf["o_ld"] = sol.n
        em.op("sync", "dma_start",
              out=AR_L[:].rearrange("p (h t) -> p h t", h=NH),
              in_=gah[:, 0:MT].rearrange("(h p) t -> p h t", p=128),
              waits=list(freeL), inc=sgl, incn=16)
        freeL.clear()
        pf["g_ld"] = sgl.n
        for hh in range(2):
            em.op("sync", "dma_start", out=qgl_s[:, hh, :],
                  in_=qgh[hh * 128:(hh + 1) * 128, 0:MT],
                  inc=sqg[hh], incn=16)
        pf["qg_ld"] = [sqg[0].n, sqg[1].n]
        em.wait("gpsimd", ccd, 1)
        Tj = [H_f32[:, 8192:10240].rearrange("p (h e) -> p h e", h=NH),
              H_f32[:, 8192:10240].rearrange("p (h e) -> p h e", h=NH)]
        Sst = H_f32[:, 10240:12288]
        tmp = H_f32[:, 12288:14336]
        em.op("vector", "memset", Sst, 0.0, waits=wH, inc=dv)
        v_z = dv.n
        mult_mark = v_z
        for j in range(4):
            ssem = scc if j % 2 == 0 else scc2
            wts = [(pe, p_all), (dv, mult_mark)]
            em.op("gpsimd", "dma_start", out=Tj[0][:, :, :],
                  in_=cco[j * 2048:(j + 1) * 2048].rearrange(
                      "(h d) e -> d h e", d=128),
                  waits=wts, inc=ssem, incn=16)
            ld_mark = (ssem, ssem.n)
            wbc = wrep[:, j, :].unsqueeze(2).broadcast_to([128, NH, HD])
            em.op("vector", "tensor_tensor",
                  tmp.rearrange("p (h e) -> p h e", h=NH),
                  Tj[0][:, :, :], wbc, ALU.mult,
                  waits=[ld_mark, (dv, setup.marks["dv"]),
                         setup.marks["wrep"]], inc=dv)
            mult_mark = dv.n
            em.op("vector", "tensor_tensor", Sst, Sst, tmp, ALU.add, inc=dv)
        last = dv.n
        em.op("vector", "tensor_copy", Ssbf.rearrange("p h e -> p (h e)"),
              Sst, waits=[(dv, last)], inc=dv)
        freeH.append((dv, dv.n))
        return dv.n, pf

    qg_free = [0, 0]
    var_mm_last = [0]

    def pass2(mt, ss_mark, pf=None):
        t0 = mt * MT
        if pf is None:
            em.wait("sync", sto[0], sto[0].n)
            em.wait("sync", sto[1], sto[1].n)
            em.wait("sync", stq[0], stq[0].n)
            em.wait("sync", stq[1], stq[1].n)
            em.wait("sync", stg[0], stg[0].n)
            em.wait("sync", stg[1], stg[1].n)
            wts = list(freeH)
            freeH.clear()
            em.op("sync", "dma_start",
                  out=H_f32[:, 0:8192].rearrange("p (h t) -> p h t", h=NH),
                  in_=oh[:, t0:t0 + MT].rearrange("(h p) t -> p h t", p=128),
                  waits=wts, inc=sol, incn=16)
            o_ld = sol.n
            wts = list(freeL)
            freeL.clear()
            em.op("sync", "dma_start",
                  out=AR_L[:].rearrange("p (h t) -> p h t", h=NH),
                  in_=gah[:, t0:t0 + MT].rearrange("(h p) t -> p h t", p=128),
                  waits=wts, inc=sgl, incn=16)
            g_ld = sgl.n
            qg_ld = [0, 0]
        else:
            o_ld = pf["o_ld"]
            g_ld = pf["g_ld"]
            qg_ld = list(pf["qg_ld"])
            freeH.clear()
            freeL.clear()

        def qg_issue(h):
            s = h % 2
            w = [(pe, qg_free[s])] if qg_free[s] else []
            em.op("sync", "dma_start", out=qgl_s[:, s, :],
                  in_=qgh[h * 128:(h + 1) * 128, t0:t0 + MT],
                  waits=w, inc=sqg[s], incn=16)
            qg_ld[s] = sqg[s].n

        if pf is None:
            qg_issue(0)
            qg_issue(1)
        em.phase = "pass2-oc"
        for h in range(NH):
            s = h % 2
            pi = s
            wts = [(sqg[s], qg_ld[s]), (dv, ss_mark)] + psfree(pi)
            em.op("tensor", "matmul", PS[pi][:], Ssbf[:, h, :], qgl_s[:, s, :],
                  start=True, stop=True, waits=wts, inc=pe)
            p_oc = pe.n
            qg_free[s] = p_oc
            if h + 2 < NH:
                qg_issue(h + 2)
            em.op("vector", "tensor_tensor", ofull(h), ofull(h), PS[pi][:],
                  ALU.add, waits=[(pe, p_oc), (sol, o_ld)], inc=dv)
            set_psfree(pi, dv, dv.n)
            em.wait("vector", dv, dv.n)
            v_of = dv.n
            osl2 = h % 2
            wsq = [(dv, v_of), (pe, var_mm_last[0])]
            if sq_war[osl2]:
                wsq.append((pe, sq_war[osl2]))
            em.op("scalar", "activation", SQ[:, osl2, :], ofull(h), AF.Square,
                  waits=wsq, inc=ac)
            a_sq = ac.n
            wts = [(ac, a_sq)] + (psfree(4) if h == 0 else [])
            em.op("tensor", "matmul", PS[4][:], ones_bf[:], SQ[:, osl2, :],
                  start=(h == 0), stop=(h == NH - 1), waits=wts, inc=pe)
            var_mm_last[0] = pe.n
            sq_war[osl2] = pe.n
        p_var = pe.n
        em.op("vector", "tensor_scalar", R2[:], PS[4][:], 1.0 / D, EPS_RMS,
              ALU.mult, ALU.add, waits=[(pe, p_var)], inc=dv)
        set_psfree(4, dv, dv.n)
        em.wait("vector", dv, dv.n)
        em.op("vector", "reciprocal", R2[:], R2[:], inc=dv)
        v_r2 = dv.n
        em.op("scalar", "activation", R2[:], R2[:], AF.Sqrt,
              waits=[(dv, v_r2)], inc=ac)
        a_R = ac.n
        z_marks = []
        for h in range(NH):
            em.op("vector", "scalar_tensor_tensor", t1_s[:], ofull(h),
                  nw_sb[:, h:h + 1], R2[:], ALU.mult, ALU.mult,
                  waits=[(ac, a_R)], inc=dv)
            em.wait("vector", dv, dv.n)
            em.op("vector", "tensor_tensor", gal(h), t1_s[:], gal(h),
                  ALU.mult, waits=[(sgl, g_ld)], inc=dv)
            z_marks.append(dv.n)
        v_z = dv.n
        freeH.append((dv, v_z))
        stW = WStream(swA, sfA, wgs, (pe, pe.n), nslots=3)

        def osrc(f):
            return wob[f]

        em.phase = "outproj"
        stW.issue(0, osrc(0), True)
        stW.issue(1, osrc(1), True)
        stW.issue(2, osrc(2), True)
        for f in range(DC):
            s = f % 3
            pi = f % 4
            wts = [stW.loaded[s]] + psfree(pi)
            for k in range(DC):
                kw = list(wts) if k == 0 else []
                if f == 0:
                    kw.append((dv, z_marks[k]))
                em.op("tensor", "matmul", PS[pi][:], wgs(s)[:, k, :], gal(k),
                      start=(k == 0), stop=(k == DC - 1),
                      waits=kw,
                      inc=(pe if k == DC - 1 else None))
            stW.consumed(f, (pe, pe.n))
            p_mm = pe.n
            if f + 3 < DC:
                stW.issue(f + 3, osrc(f + 3), True)
            sl = f % 2
            wxd = [(std[sl], std[sl].n)]
            if xd_free[sl][0]:
                wxd += [(dv, xd_free[sl][0]), (ac, xd_free[sl][1])]
            em.op("sync", "dma_start", out=xd_s[:, sl, :],
                  in_=x1h[f * 128:(f + 1) * 128, t0:t0 + MT],
                  waits=wxd, inc=sxd[sl], incn=16)
            x_ld = sxd[sl].n
            em.op("vector", "tensor_tensor", xd_s[:, sl, :], PS[pi][:],
                  xd_s[:, sl, :], ALU.add,
                  waits=[(pe, p_mm), (sxd[sl], x_ld)], inc=dv)
            set_psfree(pi, dv, dv.n)
            v_x2 = dv.n
            em.op("sync", "dma_start",
                  out=x2h[f * 128:(f + 1) * 128, t0:t0 + MT],
                  in_=xd_s[:, sl, :], waits=[(dv, v_x2)], inc=std[sl], incn=16)
            wpc = [(pe, sq_war[sl])] if sq_war[sl] else []
            if f == 0:
                wpc.append((pe, xbg_rd["pe"]))
            em.op("vector", "tensor_copy", XBG[:, f, :],
                  xd_s[:, sl, :], waits=wpc, inc=dv)
            vb = dv.n
            em.op("scalar", "activation", SQ[:, sl, :],
                  xd_s[:, sl, :], AF.Square, waits=[(dv, v_x2)] + wpc, inc=ac)
            va = ac.n
            wts = [(dv, vb)] + (psfree(4) if f == 0 else [])
            em.op("tensor", "matmul", PS[4][:], ones_bf[:],
                  XBG[:, f, :], start=(f == 0), stop=(f == DC - 1),
                  waits=wts, inc=pe)
            wts = [(ac, va)] + (psfree(5) if f == 0 else [])
            em.op("tensor", "matmul", PS[5][:], ones_bf[:],
                  SQ[:, sl, :], start=(f == 0), stop=(f == DC - 1),
                  waits=wts, inc=pe)
            sq_war[sl] = pe.n
            xd_free[sl] = (vb, va)
        stats_pre[0] = pe.n
        freeL.append((pe, pe.n))

    for pair in ((wob, wo, DC), (wgb, wg, HC), (wub, wu, HC)):
        dst_t, src_t, n = pair
        for i in range(n):
            bg_jobs.append((dst_t[i], src_t[i]))
    for i in range(DC):
        bg_jobs.append((wdb[i], wd[i]))

    setup()
    for mt in range(NMT):
        pre = ffn(mt, 0, x1h)
        attn1(mt, pre)
    ss_mark, pf0 = cc_phase()
    for mt in range(NMT):
        pass2(mt, ss_mark, pf0 if mt == 0 else None)
        ffn(mt, 1, out)
    em.wait("sync", std[0], std[0].n)
    em.wait("sync", std[1], std[1].n)

    with nc.Block() as block:
        @block.sync
        def _(e):
            em.replay("sync", e)

        @block.gpsimd
        def _(e):
            em.replay("gpsimd", e)

        @block.vector
        def _(e):
            em.replay("vector", e)

        @block.scalar
        def _(e):
            em.replay("scalar", e)

        @block.tensor
        def _(e):
            em.replay("tensor", e)

    es.close()
    nc._em_labels = em.mm_labels
    return nc


_NC_CACHE = []


def _stage_lhsT(w):
    w = np.asarray(w)
    O, K = w.shape
    wT = np.ascontiguousarray(w.T)
    v = wT.reshape(K // 128, 128, O // 128, 128)
    return np.ascontiguousarray(v.transpose(2, 1, 0, 3))


def kernel(L0_attn_hidden_states, L0_attn_slope_rate, L0_attn_w_qkv,
           L0_attn_w_gate, L0_attn_w_out, L0_attn_norm_w,
           L0_ffn_wq_gate, L0_ffn_s_gate, L0_ffn_wq_up, L0_ffn_s_up,
           L0_ffn_wq_down, L0_ffn_s_down):
    x = np.asarray(L0_attn_hidden_states)
    xs = np.ascontiguousarray(x.reshape(16384, D))

    wg_s = _stage_lhsT(L0_ffn_wq_gate)
    wu_s = _stage_lhsT(L0_ffn_wq_up)
    wdT = np.ascontiguousarray(np.asarray(L0_ffn_wq_down).T)
    v = wdT.reshape(2, 22, 128, DC, 128)
    wd_s = np.ascontiguousarray(v.transpose(3, 0, 2, 1, 4))
    wqkv_s = _stage_lhsT(L0_attn_w_qkv)
    _vb = wqkv_s[2::3]
    wv_s = np.ascontiguousarray(np.concatenate([_vb[0::2], _vb[1::2]],
                                               axis=-1))
    wga_s = _stage_lhsT(L0_attn_w_gate)
    wo_s = _stage_lhsT(L0_attn_w_out)
    sg_s = np.ascontiguousarray(
        np.asarray(L0_ffn_s_gate).reshape(HC, 128).T.astype(np.float32))
    su_s = np.ascontiguousarray(
        np.asarray(L0_ffn_s_up).reshape(HC, 128).T.astype(np.float32))
    sd_s = np.ascontiguousarray(
        np.asarray(L0_ffn_s_down).reshape(DC, 128).T.astype(np.float32))
    nw_s = np.ascontiguousarray(
        np.asarray(L0_attn_norm_w).reshape(DC, 128).T.astype(np.float32))
    sl_s = np.ascontiguousarray(
        np.asarray(L0_attn_slope_rate).reshape(1, NH).astype(np.float32))

    if not _NC_CACHE:
        _NC_CACHE.append(build_program())
    nc = _NC_CACHE[0]
    in_maps = []
    for c in range(NCORES):
        g = c % 4
        ccm_c = np.zeros((1, 4), np.float32)
        ccp_c = np.zeros((1, 4), np.float32)
        for j in range(4):
            if j < g:
                ccm_c[0, j] = 1.0
                ccp_c[0, j] = -2048.0 * (g - 1 - j)
        xT = np.ascontiguousarray(xs[c * T:(c + 1) * T].T)
        in_maps.append({
            "xin": xT, "wg": wg_s, "wu": wu_s, "wd": wd_s,
            "wqkv": wqkv_s, "wvin": wv_s, "wga": wga_s, "wo": wo_s,
            "sgate": sg_s, "sup": su_s, "sdown": sd_s, "nwin": nw_s,
            "slopes": sl_s, "ccm": ccm_c, "ccp": ccp_c,
        })
    res = run_bass_kernel_spmd(nc, in_maps, core_ids=list(range(NCORES)))
    outs = [np.asarray(res.results[c]["out"]).T for c in range(NCORES)]
    full = np.concatenate(outs, axis=0).reshape(2, 8192, D)
    return full.astype(np.float32)


# version 26
# speedup vs baseline: 1.0251x; 1.0032x over previous
import numpy as np
from contextlib import ExitStack
import concourse.bass as bass
import concourse.mybir as mybir
from concourse.bass_utils import run_bass_kernel_spmd

F32 = mybir.dt.float32
BF16 = mybir.dt.bfloat16
I32 = mybir.dt.int32
AF = mybir.ActivationFunctionType
ALU = mybir.AluOpType

NCORES = 8
T = 2048
D = 2048
DC = 16
HID = 5632
HC = 44
NH = 16
HD = 128
BLK = 256
MT = 512
NMT = T // MT
TC = MT // 128
NB = MT // BLK
EPS_LN = 1e-5
EPS_RMS = 1e-6


class Sem:
    def __init__(self, nc, es, name):
        self.h = es.enter_context(nc.semaphore(name))
        self.n = 0
        self.name = name

    def bump(self, k):
        self.n += k
        return self.n


class Em:
    ENGS = ("sync", "gpsimd", "vector", "scalar", "tensor")

    def __init__(self):
        self.q = {e: [] for e in self.ENGS}
        self.autosync = {}
        self._last_auto = {}
        self.mm_labels = []
        self.phase = "setup"

    def op(self, eng, meth, *a, waits=(), inc=None, incn=1, **kw):
        if eng == "tensor" and meth in ("matmul", "transpose"):
            self.mm_labels.append(self.phase)
        if eng in self.autosync:
            s0 = self.autosync[eng]
            if s0.n > 0 and self._last_auto.get(eng, -1) < s0.n:
                self.q[eng].append(("wait", s0, s0.n))
                self._last_auto[eng] = s0.n
        ws = [(s, v) for (s, v) in waits if v > 0]
        for (s, v) in ws:
            assert v <= s.n, f"wait on future: {s.name} {v} > {s.n} ({eng}.{meth})"
        inc.bump(incn) if inc is not None else None
        self.q[eng].append(("op", meth, a, kw, ws, inc, incn))

    def wait(self, eng, sem, val):
        if val <= 0:
            return
        assert val <= sem.n, f"wait on future: {sem.name} {val} > {sem.n}"
        self.q[eng].append(("wait", sem, val))

    def replay(self, eng_name, e):
        for item in self.q[eng_name]:
            if item[0] == "wait":
                _, sem, val = item
                e.wait_ge(sem.h, val)
            else:
                _, meth, a, kw, ws, inc, incn = item
                for (s, v) in ws:
                    e.wait_ge(s.h, v)
                instr = getattr(e, meth)(*a, **kw)
                if inc is not None:
                    instr.then_inc(inc.h, incn)


def build_program():
    nc = bass.Bass("TRN2", target_bir_lowering=False)
    es = ExitStack()

    xin = nc.dram_tensor("xin", [D, T], F32, kind="ExternalInput").ap()
    wg = nc.dram_tensor("wg", [HC, 128, DC, 128], I32, kind="ExternalInput").ap()
    wu = nc.dram_tensor("wu", [HC, 128, DC, 128], I32, kind="ExternalInput").ap()
    wd = nc.dram_tensor("wd", [DC, 2, 128, 22, 128], I32, kind="ExternalInput").ap()
    wqkv = nc.dram_tensor("wqkv", [48, 128, DC, 128], F32, kind="ExternalInput").ap()
    wvin = nc.dram_tensor("wvin", [8, 128, DC, 256], F32, kind="ExternalInput").ap()
    wga = nc.dram_tensor("wga", [DC, 128, DC, 128], F32, kind="ExternalInput").ap()
    wo = nc.dram_tensor("wo", [DC, 128, DC, 128], F32, kind="ExternalInput").ap()
    sgate = nc.dram_tensor("sgate", [128, HC], F32, kind="ExternalInput").ap()
    sup = nc.dram_tensor("sup", [128, HC], F32, kind="ExternalInput").ap()
    sdown = nc.dram_tensor("sdown", [128, DC], F32, kind="ExternalInput").ap()
    nwin = nc.dram_tensor("nwin", [128, DC], F32, kind="ExternalInput").ap()
    slopes = nc.dram_tensor("slopes", [1, NH], F32, kind="ExternalInput").ap()
    ccm = nc.dram_tensor("ccm", [1, 4], F32, kind="ExternalInput").ap()
    ccp = nc.dram_tensor("ccp", [1, 4], F32, kind="ExternalInput").ap()
    out = nc.dram_tensor("out", [D, T], F32, kind="ExternalOutput").ap()

    x1h = nc.dram_tensor("x1h", [D, T], F32, kind="Internal").ap()
    x2h = nc.dram_tensor("x2h", [D, T], F32, kind="Internal").ap()
    oh = nc.dram_tensor("oh", [D, T], F32, kind="Internal").ap()
    qgh = nc.dram_tensor("qgh", [D, T], BF16, kind="Internal").ap()
    gah = nc.dram_tensor("gah", [D, T], BF16, kind="Internal").ap()
    wgb = nc.dram_tensor("wgb", [HC, 128, DC, 128], BF16, kind="Internal").ap()
    wub = nc.dram_tensor("wub", [HC, 128, DC, 128], BF16, kind="Internal").ap()
    wdb = nc.dram_tensor("wdb", [DC, 2, 128, 22, 128], BF16, kind="Internal").ap()
    wqkvb = nc.dram_tensor("wqkvb", [48, 128, DC, 128], BF16, kind="Internal").ap()
    wvb = nc.dram_tensor("wvb", [8, 128, DC, 256], BF16, kind="Internal").ap()
    wgab = nc.dram_tensor("wgab", [DC, 128, DC, 128], BF16, kind="Internal").ap()
    wob = nc.dram_tensor("wob", [DC, 128, DC, 128], BF16, kind="Internal").ap()
    cci = nc.dram_tensor("cci", [NH * HD, HD], F32, kind="Internal").ap()
    ccs_d = nc.dram_tensor("ccs_d", [1, 4 * NH], F32, kind="Internal").ap()
    cco = nc.dram_tensor("cco", [4 * NH * HD, HD], F32, kind="Internal").ap()

    def sb(name, shape, dt):
        return es.enter_context(nc.sbuf_tensor(name, shape, dt))

    AR_H = sb("AR_H", [128, 32768], BF16)
    AR_X = sb("AR_X", [128, 8192], BF16)
    AR_L = sb("AR_L", [128, 8192], BF16)
    AR_W = sb("AR_W", [128, 23552], BF16)

    L_sb = sb("L_sb", [128, NH, HD], F32)
    Lbf = sb("Lbf", [128, 2, HD], BF16)
    s128n = sb("s128n", [128, MT], F32)
    shn = sb("shn", [128, MT], F32)
    mbase = sb("mbase", [128, BLK], BF16)
    diffc = sb("diffc", [128, 2, BLK], BF16)
    ddT = sb("ddT", [128, 2, 2, BLK], BF16)
    qdec = sb("qdec", [128, 2, BLK], BF16)
    kdecT = sb("kdecT", [128, NH, 2], F32)
    nvals = sb("nvals", [128, 2], F32)
    Db = sb("Db", [128, 8, NH], F32)
    srep = sb("srep", [128, NH], F32)
    snegrep = sb("snegrep", [128, NH], F32)
    wrep = sb("wrep", [128, 4, NH], F32)
    slop = sb("slop", [1, NH], F32)
    ccw = sb("ccw", [1, 4, NH], F32)
    ccm_sb = sb("ccm_sb", [1, 4], F32)
    ccp_sb = sb("ccp_sb", [1, 4], F32)
    ones_bf = sb("ones_bf", [128, 128], BF16)
    ident = sb("ident", [128, 128], BF16)
    sg_sb = sb("sg_sb", [128, HC], F32)
    su_sb = sb("su_sb", [128, HC], F32)
    sdh_sb = sb("sdh_sb", [128, DC], F32)
    nw_sb = sb("nw_sb", [128, DC], F32)
    Mt = sb("Mt", [128, 512], F32)
    Rt = sb("Rt", [128, 512], F32)
    tg_s = sb("tg_s", [128, 512], F32)
    t1_s = sb("t1_s", [128, 512], F32)
    XBG = sb("XBG", [128, DC, MT], BF16)
    SQ = sb("SQ", [128, 2, MT], BF16)
    sg_scr = sb("sg_scr", [128, 2, 512], BF16)
    qs_s = sb("qs_s", [128, 2, BLK], BF16)
    qg_s = sb("qg_s", [128, 2, 512], BF16)
    qkm_s = sb("qkm_s", [128, 2, 2, BLK], BF16)
    kT_s = sb("kT_s", [128, 2, 2, 128], BF16)
    o_s = sb("o_s", [128, 2, 512], F32)
    ga_s = sb("ga_s", [128, 2, 512], BF16)
    xd_s = sb("xd_s", [128, 2, 512], F32)
    qgl_s = sb("qgl_s", [128, 2, 512], BF16)

    M2 = tg_s
    R2 = Rt
    tu_s = t1_s
    Ssbf = AR_H[:, 28672:30720].rearrange("p (h e) -> p h e", h=NH)

    def psum(name, shape, dt):
        return es.enter_context(nc.psum_tensor(name, shape, dt))

    PS = [psum(f"PS{i}", [128, 512], F32) for i in range(7)]
    PTR = psum("PTR", [128, 512], BF16)
    PS6B = PS[6][:].bitcast(BF16)

    def S(name):
        return Sem(nc, es, name)

    pe, dv, ac, gp = S("pe"), S("dv"), S("ac"), S("gp")
    sx = [S("sx0"), S("sx1")]
    sxd = [S("sxd0"), S("sxd1")]
    sqg = [S("sqg0"), S("sqg1")]
    sol, sgl = S("sol"), S("sgl")
    swA = [S("swA0"), S("swA1"), S("swA2")]
    swB = [S("swB0"), S("swB1"), S("swB2")]
    swC = [S("swC0"), S("swC1")]
    sfA = [S("sfA0"), S("sfA1"), S("sfA2")]
    sfB = [S("sfB0"), S("sfB1"), S("sfB2")]
    sfC = [S("sfC0"), S("sfC1")]
    sbg = S("sbg")
    sto = [S("sto0"), S("sto1")]
    stq = [S("stq0"), S("stq1")]
    stg = [S("stg0"), S("stg1")]
    std = [S("std0"), S("std1")]
    scc = S("scc")
    scc2 = S("scc2")
    srw = S("srw")
    swb = S("swb")
    ccd = S("ccd")
    scst = S("scst")

    em = Em()
    em.autosync = {"vector": dv, "scalar": ac}

    H_f32 = AR_H[:].bitcast(F32)
    AXf = AR_X[:].bitcast(F32)

    def Hchunk(h):
        return AR_H[:, h * MT:(h + 1) * MT]

    def qv(h):
        return AR_H[:, h * MT:(h + 1) * MT]

    def kv_(h):
        return AR_H[:, 8192 + h * MT:8192 + (h + 1) * MT]

    def ofull(h):
        return H_f32[:, h * MT:(h + 1) * MT]

    def vT(tc):
        return AR_X[:, tc * 2048:(tc + 1) * 2048]

    def lnc(k):
        return AR_L[:, k * MT:(k + 1) * MT]

    def gal(h):
        return AR_L[:, h * MT:(h + 1) * MT]

    def wgs(s):
        return AR_W[:, s * 2048:(s + 1) * 2048].rearrange("p (c o) -> p c o", c=DC)

    def wus(s):
        return AR_W[:, 6144 + s * 2048:6144 + (s + 1) * 2048].rearrange(
            "p (c o) -> p c o", c=DC)

    def wvs(s):
        return AR_W[:, 6144 + s * 4096:6144 + (s + 1) * 4096].rearrange(
            "p (c o) -> p c o", c=DC)

    def wds(s):
        return AR_W[:, 12288 + s * 5632:12288 + (s + 1) * 5632].rearrange(
            "p (a c o) -> p a c o", a=2, c=22)

    psum_free = [0] * 8
    psum_free_sem = [dv] * 8
    sq_war = [0, 0]
    xbg_rd = {"pe": 0, "dv": 0}
    war = {"a_sg": 0, "a_m2": 0, "a_r": 0}
    freeH = []
    freeL = []
    freeX = []
    bg_jobs = []
    bg_total = [0]
    wbmarks = {}

    def bg_emit(k):
        while bg_jobs and k > 0:
            dst_ap, src_ap = bg_jobs.pop(0)
            em.op("gpsimd", "dma_start", out=dst_ap, in_=src_ap,
                  inc=sbg, incn=16)
            k -= 1
        bg_total[0] = sbg.n

    def psfree(i):
        return [(psum_free_sem[i], psum_free[i])] if psum_free[i] else []

    def set_psfree(i, sem, val):
        psum_free_sem[i] = sem
        psum_free[i] = val

    def setup():
        em.op("sync", "dma_start", out=sg_sb[:], in_=sgate, inc=scst, incn=16)
        em.op("sync", "dma_start", out=su_sb[:], in_=sup, inc=scst, incn=16)
        em.op("sync", "dma_start", out=sdh_sb[:], in_=sdown, inc=scst, incn=16)
        em.op("sync", "dma_start", out=nw_sb[:], in_=nwin, inc=scst, incn=16)
        em.op("sync", "dma_start", out=slop[:], in_=slopes, inc=scst, incn=16)
        em.op("sync", "dma_start", out=ccm_sb[:], in_=ccm, inc=scst, incn=16)
        em.op("sync", "dma_start", out=ccp_sb[:], in_=ccp, inc=scst, incn=16)
        ldall = (scst, scst.n)

        em.op("vector", "memset", ones_bf[:], 1.0, inc=dv)
        em.op("vector", "memset", L_sb[:].rearrange("p h e -> p (h e)"), 0.0,
              inc=dv)
        v_ones = dv.n
        em.op("vector", "tensor_scalar_mul", sdh_sb[:], sdh_sb[:], 0.5,
              waits=[ldall], inc=dv)
        em.op("gpsimd", "affine_select", ident[:], ones_bf[:],
              pattern=[[1, 128]], compare_op=ALU.is_equal, fill=0.0, base=0,
              channel_multiplier=-1, waits=[(dv, v_ones)], inc=gp)
        em.op("gpsimd", "iota", mbase[:], pattern=[[1, BLK]], base=1,
              channel_multiplier=0, allow_small_or_imprecise_dtypes=True,
              inc=gp)
        em.op("gpsimd", "iota", diffc[:], pattern=[[-128, 2], [1, BLK]],
              base=0, channel_multiplier=-1,
              allow_small_or_imprecise_dtypes=True, inc=gp)
        em.op("gpsimd", "iota", nvals[:], pattern=[[128, 2]], base=-255,
              channel_multiplier=1, allow_small_or_imprecise_dtypes=True,
              inc=gp)
        g_iota = gp.n
        em.op("vector", "tensor_single_scalar", ddT[:, 0], diffc[:], 0.0,
              ALU.is_lt, waits=[(gp, g_iota)], inc=dv)
        em.op("vector", "tensor_scalar_max", diffc[:], diffc[:], 0.0, inc=dv)
        em.op("vector", "scalar_tensor_tensor", diffc[:], ddT[:, 0], 1.0e6,
              diffc[:], ALU.mult, ALU.add, inc=dv)
        em.op("sync", "dma_start", out=srep[:],
              in_=slopes.broadcast_to([128, NH]), waits=[ldall],
              inc=scst, incn=16)
        g_rep = 0
        rep_ld = (scst, scst.n)
        em.op("vector", "tensor_scalar_mul", snegrep[:], srep[:], -1.0,
              waits=[rep_ld], inc=dv)
        v_srep = dv.n
        for b in range(8):
            em.op("scalar", "activation", Db[:, b, :], srep[:], AF.Exp,
                  scale=float(-256.0 * b),
                  waits=[rep_ld, (dv, v_srep)], inc=ac)
        for h in range(NH):
            em.op("scalar", "activation", kdecT[:, h, :], nvals[:], AF.Exp,
                  scale=srep[:, h:h + 1],
                  waits=[(dv, v_srep), (gp, g_iota), rep_ld], inc=ac)
        for j in range(4):
            em.op("scalar", "activation", ccw[:, j, :], slop[:], AF.Exp,
                  scale=ccp_sb[:, j:j + 1], waits=[ldall], inc=ac)
        a_ccw = ac.n
        for j in range(4):
            em.op("vector", "tensor_scalar_mul", ccw[:, j, :], ccw[:, j, :],
                  ccm_sb[:, j:j + 1], waits=[(ac, a_ccw)], inc=dv)
        v_ccw = dv.n
        em.op("gpsimd", "dma_start", out=ccs_d,
              in_=ccw[:].rearrange("p j h -> p (j h)"),
              waits=[(dv, v_ccw)], inc=srw, incn=16)
        em.op("gpsimd", "dma_start",
              out=wrep[:].rearrange("p j h -> p (j h)"),
              in_=ccs_d.broadcast_to([128, 4 * NH]),
              waits=[(srw, srw.n)], inc=srw, incn=16)
        setup.marks = {"dv": dv.n, "ac": ac.n, "gp": gp.n,
                       "wrep": (srw, srw.n)}

    def stats_chunk(k, xsrc_ap, ld_wait, sq_extra=()):
        sl = k % 2
        wq = list(ld_wait) + list(sq_extra)
        if sq_war[sl]:
            wq.append((pe, sq_war[sl]))
        em.op("scalar", "activation", SQ[:, sl, :], xsrc_ap, AF.Square,
              waits=wq, inc=ac)
        va = ac.n
        wts = list(ld_wait) + (psfree(4) if k == 0 else [])
        em.op("tensor", "matmul", PS[4][:], ones_bf[:], xsrc_ap,
              start=(k == 0), stop=(k == DC - 1), waits=wts, inc=pe)
        wts = [(ac, va)] + (psfree(5) if k == 0 else [])
        em.op("tensor", "matmul", PS[5][:], ones_bf[:], SQ[:, sl, :],
              start=(k == 0), stop=(k == DC - 1), waits=wts, inc=pe)
        sq_war[sl] = pe.n

    def layernorm(mt, src, want_s, pre=None):
        em.phase = f"LN(pre={pre is not None})"
        """LN over features, input resident in XBG (bf16). If pre is None,
        fill XBG from HBM src and accumulate stats; else pre is the pe-mark
        of already-accumulated stats in PS4 (sum) / PS5 (sumsq)."""
        t0 = mt * MT
        if pre is None:
            for k in range(DC):
                sl = k % 2
                wxd = [(std[sl], std[sl].n)]
                if xd_free[sl][0]:
                    wxd += [(dv, xd_free[sl][0]), (ac, xd_free[sl][1])]
                em.op("sync", "dma_start", out=xd_s[:, sl, :],
                      in_=src[k * 128:(k + 1) * 128, t0:t0 + MT],
                      waits=wxd, inc=sxd[sl], incn=16)
                x_ld = sxd[sl].n
                w0 = [(sxd[sl], x_ld)]
                if k == 0:
                    w0.append((pe, xbg_rd["pe"]))
                em.op("vector", "tensor_copy", XBG[:, k, :], xd_s[:, sl, :],
                      waits=w0, inc=dv)
                vb = dv.n
                wq = [(sxd[sl], x_ld)]
                if sq_war[sl]:
                    wq.append((pe, sq_war[sl]))
                em.op("scalar", "activation", SQ[:, sl, :], xd_s[:, sl, :],
                      AF.Square, waits=wq, inc=ac)
                va = ac.n
                wts = [(dv, vb)] + (psfree(4) if k == 0 else [])
                em.op("tensor", "matmul", PS[4][:], ones_bf[:], XBG[:, k, :],
                      start=(k == 0), stop=(k == DC - 1), waits=wts, inc=pe)
                wts = [(ac, va)] + (psfree(5) if k == 0 else [])
                em.op("tensor", "matmul", PS[5][:], ones_bf[:], SQ[:, sl, :],
                      start=(k == 0), stop=(k == DC - 1), waits=wts, inc=pe)
                sq_war[sl] = pe.n
                xd_free[sl] = (vb, va)
            p_stats = pe.n
        else:
            p_stats = pre
        em.op("vector", "tensor_scalar_mul", Mt[:], PS[4][:], 1.0 / D,
              waits=[(pe, p_stats), (ac, war["a_m2"])], inc=dv)
        set_psfree(4, dv, dv.n)
        v_m = dv.n
        em.op("scalar", "activation", M2[:], Mt[:], AF.Square,
              waits=[(dv, v_m)], inc=ac)
        a_m2 = ac.n
        war["a_m2"] = a_m2
        em.op("vector", "scalar_tensor_tensor", Rt[:], PS[5][:], 1.0 / D,
              M2[:], ALU.mult, ALU.subtract,
              waits=[(ac, a_m2), (ac, war["a_r"])], inc=dv)
        set_psfree(5, dv, dv.n)
        em.op("vector", "tensor_scalar_add", Rt[:], Rt[:], EPS_LN, inc=dv)
        em.op("vector", "reciprocal", Rt[:], Rt[:], inc=dv)
        v_v = dv.n
        em.op("scalar", "activation", Rt[:], Rt[:], AF.Sqrt,
              waits=[(dv, v_v)], inc=ac)
        a_r = ac.n
        war["a_r"] = a_r
        first_write = True
        nmarks = []
        ps_started = [False]
        for cg in range(4):
            Mbc = Mt[:].unsqueeze(1).broadcast_to([128, 4, 512])
            Rbc = Rt[:].unsqueeze(1).broadcast_to([128, 4, 512])
            dstv = AR_L[:, cg * 2048:(cg + 1) * 2048].rearrange(
                "p (c t) -> p c t", c=4)
            wl = [(pe, p_stats)] if cg == 0 else []
            if first_write:
                wl += list(freeL)
                freeL.clear()
                first_write = False
            em.op("vector", "tensor_tensor", dstv,
                  XBG[:, cg * 4:(cg + 1) * 4, :], Mbc, ALU.subtract,
                  waits=wl, inc=dv)
            em.op("vector", "tensor_tensor", dstv, dstv, Rbc, ALU.mult,
                  waits=([(ac, a_r)] if cg == 0 else []), inc=dv)
            nmarks.append(dv.n)
            if want_s:
                for j in range(4):
                    k = cg * 4 + j
                    wts = [(dv, nmarks[-1])] if j == 0 else []
                    if k == 0:
                        wts = wts + psfree(4)
                    em.op("tensor", "matmul", PS[4][:], ones_bf[:], lnc(k),
                          start=(k == 0), stop=(k == DC - 1), waits=wts,
                          inc=pe)
        s_mark = 0
        if want_s:
            p_s = pe.n
            em.op("vector", "tensor_scalar_mul", s128n[:], PS[4][:], -128.0,
                  waits=[(pe, p_s)], inc=dv)
            set_psfree(4, dv, dv.n)
            s_mark = dv.n
        xbg_rd["pe"] = p_stats
        xbg_rd["dv"] = nmarks[-1]
        return nmarks, s_mark

    class WStream:
        def __init__(self, sems_cast, sems_bf, slot_fn, init_free, nslots=2,
                     gate=None):
            self.sc, self.sf = sems_cast, sems_bf
            self.slot = slot_fn
            self.ns = nslots
            self.freeat = [init_free] * nslots
            self.loaded = [None] * nslots
            self.wb = [(swb, swb.n) if swb.n else None] * nslots
            self.gate = gate

        def issue(self, i, src_ap, cached, dst=None, bg=0, wb_dst=None):
            s = i % self.ns
            w = [self.freeat[s]] if self.freeat[s] is not None else []
            if self.wb[s] is not None:
                w.append(self.wb[s])
                self.wb[s] = None
            tgt = dst if dst is not None else self.slot(s)[:]
            if cached:
                w.append(self.gate if self.gate else (sbg, bg_total[0]))
                em.op("scalar", "dma_start", out=tgt, in_=src_ap,
                      waits=w, inc=self.sf[s], incn=16)
                self.loaded[s] = (self.sf[s], self.sf[s].n)
            else:
                em.op("gpsimd", "dma_start", out=tgt, in_=src_ap,
                      waits=w, inc=self.sc[s], incn=16)
                self.loaded[s] = (self.sc[s], self.sc[s].n)
                if wb_dst is not None:
                    em.op("sync", "dma_start", out=wb_dst, in_=tgt,
                          waits=[self.loaded[s]], inc=swb, incn=16)
                    self.wb[s] = (swb, swb.n)
            if bg:
                bg_emit(bg)

        def consumed(self, i, mark):
            self.freeat[i % self.ns] = mark

    def ffn(mt, call, dst):
        t0 = mt * MT
        sgH_mark = ffn.sgH

        cached = (call == 1) or mt >= 2
        pmark = (pe, pe.n)
        stG = WStream(swA, sfA, wgs, pmark, nslots=3)
        stU = WStream(swB, sfB, wus, pmark, nslots=3)
        stD = WStream(swC, sfC, wds, pmark)

        def gsrc(h):
            return (wgb if cached else wg)[h].rearrange("p c o -> p c o")

        def usrc(h):
            return (wub if cached else wu)[h].rearrange("p c o -> p c o")

        def dsrc(d):
            return (wdb if cached else wd)[d].rearrange(
                "d2 p c o -> p d2 c o")

        for i0 in range(3):
            stG.issue(i0, gsrc(i0), cached)
            stU.issue(i0, usrc(i0), cached)
        nmarks, s_mark = layernorm(mt, xin if call == 0 else x2h, True,
                                   stats_pre[0])
        stats_pre[0] = None
        ln_done = nmarks[3]
        em.phase = f"ffn{call}-gateup"
        h_last = 0
        first_H = True
        sh_q = []
        for h in range(HC):
            s = h % 3
            gi, ui = h % 2, 2 + h % 2
            wts = [stG.loaded[s]] + psfree(gi)
            if h > 0:
                wts.append((dv, ln_done))
            for k in range(DC):
                kw = wts if k == 0 else []
                if h == 0:
                    kw = list(kw) + [(dv, nmarks[k // 4])]
                em.op("tensor", "matmul", PS[gi][:], wgs(s)[:, k, :], lnc(k),
                      start=(k == 0), stop=(k == DC - 1),
                      waits=kw,
                      inc=(pe if k == DC - 1 else None))
            stG.consumed(h, (pe, pe.n))
            p_g = pe.n
            wts = [stU.loaded[s]] + psfree(ui)
            for k in range(DC):
                kw = wts if k == 0 else []
                if h == 0:
                    kw = list(kw) + [(dv, nmarks[k // 4])]
                em.op("tensor", "matmul", PS[ui][:], wus(s)[:, k, :], lnc(k),
                      start=(k == 0), stop=(k == DC - 1),
                      waits=kw,
                      inc=(pe if k == DC - 1 else None))
            stU.consumed(h, (pe, pe.n))
            p_u = pe.n
            if h + 3 < HC:
                stG.issue(h + 3, gsrc(h + 3), cached)
                stU.issue(h + 3, usrc(h + 3), cached)
            if h == 36:
                stD.issue(0, dsrc(0), cached)
                stD.issue(1, dsrc(1), cached)
            em.op("vector", "tensor_tensor", tg_s[:], PS[gi][:], s128n[:],
                  ALU.add,
                  waits=[(pe, p_g), (dv, s_mark), (ac, war["a_sg"])], inc=dv)
            set_psfree(gi, dv, dv.n)
            v_tg = dv.n
            s2 = h % 2
            wsg = [(dv, v_tg)]
            if sgH_mark[s2]:
                wsg.append((dv, sgH_mark[s2]))
            em.op("scalar", "activation", sg_scr[:, s2, :], tg_s[:], AF.Silu,
                  scale=sg_sb[:, h:h + 1], waits=wsg, inc=ac)
            a_sg = ac.n
            war["a_sg"] = a_sg
            em.op("vector", "tensor_tensor", tu_s[:], PS[ui][:], s128n[:],
                  ALU.add, waits=[(pe, p_u)], inc=dv)
            set_psfree(ui, dv, dv.n)
            em.wait("vector", dv, dv.n)
            wl = list(freeH) if first_H else []
            first_H = False
            freeH.clear()
            em.op("vector", "scalar_tensor_tensor", Hchunk(h), tu_s[:],
                  su_sb[:, h:h + 1], sg_scr[:, s2, :], ALU.mult, ALU.mult,
                  waits=[(ac, a_sg)] + wl, inc=dv)
            h_last = dv.n
            sgH_mark[s2] = h_last
            sh_q.append((h, h_last))
            while len(sh_q) > 6:
                hh, hm = sh_q.pop(0)
                wts = [(dv, hm)] + (psfree(6) if hh == 0 else [])
                em.op("tensor", "matmul", PS[6][:], ones_bf[:], Hchunk(hh),
                      start=(hh == 0), stop=(hh == HC - 1), waits=wts, inc=pe)
        while sh_q:
            hh, hm = sh_q.pop(0)
            wts = [(dv, hm)] + (psfree(6) if hh == 0 else [])
            em.op("tensor", "matmul", PS[6][:], ones_bf[:], Hchunk(hh),
                  start=(hh == 0), stop=(hh == HC - 1), waits=wts, inc=pe)
        p_sh = pe.n
        em.op("vector", "tensor_scalar_mul", shn[:], PS[6][:], -128.0,
              waits=[(pe, p_sh)], inc=dv)
        set_psfree(6, dv, dv.n)
        v_shn = dv.n
        em.phase = f"ffn{call}-down"
        xsrc = xin if call == 0 else x2h
        pre_mark = None
        stats_q = []
        for d in range(DC):
            sl = d % 2
            wxd = [(std[sl], std[sl].n)]
            if xd_free[sl][0]:
                wxd += [(dv, xd_free[sl][0]), (ac, xd_free[sl][1])]
            em.op("sync", "dma_start", out=xd_s[:, sl, :],
                  in_=xsrc[d * 128:(d + 1) * 128, t0:t0 + MT],
                  waits=wxd, inc=sxd[sl], incn=16)
            x_ld = sxd[sl].n
            pi = d % 4
            for j in range(HC):
                wts = []
                if j == 0:
                    wts = psfree(pi) + [stD.loaded[d % 2], (dv, h_last)]
                em.op("tensor", "matmul", PS[pi][:],
                      wds(d % 2)[:, j // 22, j % 22, :],
                      Hchunk(j), start=(j == 0), stop=(j == HC - 1),
                      waits=wts, inc=(pe if j == HC - 1 else None))
            p_d = pe.n
            while len(stats_q) > 1:
                dd, ssl, vb, va = stats_q.pop(0)
                wts = [(dv, vb)] + (psfree(4) if dd == 0 else [])
                em.op("tensor", "matmul", PS[4][:], ones_bf[:],
                      XBG[:, dd, :], start=(dd == 0), stop=(dd == DC - 1),
                      waits=wts, inc=pe)
                wts = [(ac, va)] + (psfree(5) if dd == 0 else [])
                em.op("tensor", "matmul", PS[5][:], ones_bf[:],
                      SQ[:, ssl, :], start=(dd == 0), stop=(dd == DC - 1),
                      waits=wts, inc=pe)
                sq_war[ssl] = pe.n
            if d + 2 < DC:
                stD.consumed(d, (pe, p_d))
                stD.issue(d + 2, dsrc(d + 2), cached)
            em.op("vector", "tensor_tensor", t1_s[:], PS[pi][:], shn[:],
                  ALU.add, waits=[(pe, p_d), (dv, v_shn)], inc=dv)
            set_psfree(pi, dv, dv.n)
            em.op("vector", "scalar_tensor_tensor", xd_s[:, sl, :], t1_s[:],
                  sdh_sb[:, d:d + 1], xd_s[:, sl, :], ALU.mult, ALU.add,
                  waits=[(sxd[sl], x_ld)], inc=dv)
            v_x1 = dv.n
            em.op("sync", "dma_start",
                  out=dst[d * 128:(d + 1) * 128, t0:t0 + MT],
                  in_=xd_s[:, sl, :], waits=[(dv, v_x1)], inc=std[sl], incn=16)
            if call == 0:
                wpc = [(pe, sq_war[sl])] if sq_war[sl] else []
                if d == 0:
                    wpc.append((pe, xbg_rd["pe"]))
                em.op("vector", "tensor_copy", XBG[:, d, :],
                      xd_s[:, sl, :], waits=wpc, inc=dv)
                vb = dv.n
                em.op("scalar", "activation", SQ[:, sl, :],
                      xd_s[:, sl, :], AF.Square,
                      waits=[(dv, v_x1)] + wpc, inc=ac)
                va = ac.n
                stats_q.append((d, sl, vb, va))
                xd_free[sl] = (vb, va)
        while stats_q:
            dd, ssl, vb, va = stats_q.pop(0)
            wts = [(dv, vb)] + (psfree(4) if dd == 0 else [])
            em.op("tensor", "matmul", PS[4][:], ones_bf[:],
                  XBG[:, dd, :], start=(dd == 0), stop=(dd == DC - 1),
                  waits=wts, inc=pe)
            wts = [(ac, va)] + (psfree(5) if dd == 0 else [])
            em.op("tensor", "matmul", PS[5][:], ones_bf[:],
                  SQ[:, ssl, :], start=(dd == 0), stop=(dd == DC - 1),
                  waits=wts, inc=pe)
            sq_war[ssl] = pe.n
        if call == 0:
            pre_mark = pe.n
        freeH.append((pe, pe.n))
        freeL.append((pe, pe.n))
        return pre_mark

    ffn.sgH = [0, 0]
    stats_pre = [None]
    xd_free = [(0, 0), (0, 0)]

    scan = {"o_ctr": 0, "qk_ctr": 0, "dd_free": [0, 0], "qs_free": [0, 0],
            "qkm_free": [0, 0], "kT_free": [0, 0], "tr_free": [0, 0],
            "o_free": [0, 0], "oS_free": [0, 0], "Lb_free": [0, 0],
            "Lb_mark": [0, 0]}
    ga_free = [0, 0]
    qg_st_free = [0, 0]

    def attn1(mt, pre):
        attn1.lmark = getattr(attn1, "lmark", [0] * NH)
        t0 = mt * MT
        cached = mt >= 1
        agate = (swb, wbmarks["attn"]) if cached else None
        pmark = (pe, pe.n)
        nmarks, _ = layernorm(mt, x1h, False, pre)
        ln_done = nmarks[3]
        stQ = WStream(swA, sfA, wgs, pmark, nslots=3, gate=agate)
        qk_list = []
        for h in range(NH):
            qk_list.append((3 * h, True, h))
            qk_list.append((3 * h + 1, False, h))

        def qsrc(o):
            return (wqkvb if cached else wqkv)[o].rearrange("p c o -> p c o")

        def qwb(o):
            return None if cached else wqkvb[o].rearrange("p c o -> p c o")

        for i0 in range(3):
            stQ.issue(i0, qsrc(qk_list[i0][0]), cached,
                      wb_dst=qwb(qk_list[i0][0]))
        em.phase = "qkproj"
        firstH = True
        for i, (o, is_q, h) in enumerate(qk_list):
            s = i % 3
            pi = i % 4
            wts = [stQ.loaded[s]] + psfree(pi)
            if i > 0:
                wts.append((dv, ln_done))
            for k in range(DC):
                kw = wts if k == 0 else []
                if i == 0:
                    kw = list(kw) + [(dv, nmarks[k // 4])]
                em.op("tensor", "matmul", PS[pi][:], wgs(s)[:, k, :], lnc(k),
                      start=(k == 0), stop=(k == DC - 1),
                      waits=kw,
                      inc=(pe if k == DC - 1 else None))
            stQ.consumed(i, (pe, pe.n))
            p_mm = pe.n
            if i + 3 < len(qk_list):
                stQ.issue(i + 3, qsrc(qk_list[i + 3][0]), cached, bg=2,
                          wb_dst=qwb(qk_list[i + 3][0]))
            wts = [(pe, p_mm)]
            if firstH:
                wts += freeH
                freeH.clear()
                firstH = False
            em.op("scalar", "activation", qv(h) if is_q else kv_(h),
                  PS[pi][:], AF.Silu, waits=wts, inc=ac)
            set_psfree(pi, ac, ac.n)
        vload = [None, None]
        vfree = [pmark, pmark]
        v_wb = [(swb, swb.n) if swb.n else None] * 2

        def v_issue(g):
            s = g % 2
            w = [vfree[s]] if vfree[s] else []
            if v_wb[s] is not None:
                w.append(v_wb[s])
                v_wb[s] = None
            if cached:
                w.append(agate)
                em.op("scalar", "dma_start", out=wvs(s)[:, :, :],
                      in_=wvb[g], waits=w, inc=sfB[s], incn=16)
                vload[s] = (sfB[s], sfB[s].n)
            else:
                em.op("gpsimd", "dma_start", out=wvs(s)[:, :, :],
                      in_=wvin[g], waits=w, inc=swB[s], incn=16)
                vload[s] = (swB[s], swB[s].n)
                em.op("sync", "dma_start", out=wvb[g],
                      in_=wvs(s)[:, :, :], waits=[vload[s]],
                      inc=swb, incn=16)
                v_wb[s] = (swb, swb.n)
            bg_emit(2)

        em.phase = "vproj"
        v_issue(0)
        v_issue(1)
        firstX = True
        for g in range(NH // 2):
            s = g % 2
            for tc in range(TC):
                pi = (g * TC + tc) % 4
                wts = [vload[s]] + psfree(pi)
                for k in range(DC):
                    em.op("tensor", "matmul", PS[pi][:, 0:256],
                          lnc(k)[:, tc * 128:(tc + 1) * 128], wvs(s)[:, k, :],
                          start=(k == 0), stop=(k == DC - 1),
                          waits=wts if k == 0 else (),
                          inc=(pe if k == DC - 1 else None))
                p_v = pe.n
                wts2 = [(pe, p_v)]
                if firstX:
                    wts2 += freeX
                    freeX.clear()
                    firstX = False
                em.op("scalar", "activation",
                      vT(tc)[:, g * 256:(g + 1) * 256], PS[pi][:, 0:256],
                      AF.Silu, waits=wts2, inc=ac)
                set_psfree(pi, ac, ac.n)
            vfree[s] = (pe, pe.n)
            if g + 2 < NH // 2:
                v_issue(g + 2)
        def gasrc(o):
            return (wgab if cached else wga)[o]

        def gawb(o):
            return None if cached else wgab[o]

        em.phase = "gateproj"
        nq = len(qk_list)
        stQ.issue(nq, gasrc(0), cached, wb_dst=gawb(0))
        stQ.issue(nq + 1, gasrc(1), cached, wb_dst=gawb(1))
        stQ.issue(nq + 2, gasrc(2), cached, wb_dst=gawb(2))
        for o in range(DC):
            s = (nq + o) % 3
            pi = o % 4
            wts = [stQ.loaded[s], (dv, ln_done)] + psfree(pi)
            for k in range(DC):
                em.op("tensor", "matmul", PS[pi][:], wgs(s)[:, k, :], lnc(k),
                      start=(k == 0), stop=(k == DC - 1),
                      waits=wts if k == 0 else (),
                      inc=(pe if k == DC - 1 else None))
            stQ.consumed(nq + o, (pe, pe.n))
            p_mm = pe.n
            if o + 3 < DC:
                stQ.issue(nq + o + 3, gasrc(o + 3), cached, bg=2,
                          wb_dst=gawb(o + 3))
            gs = o % 2
            w2 = [(pe, p_mm)]
            if ga_free[gs]:
                w2.append((stg[gs], ga_free[gs]))
            em.op("scalar", "activation", ga_s[:, gs, :], PS[pi][:],
                  AF.Sigmoid, waits=w2, inc=ac)
            set_psfree(pi, ac, ac.n)
            a_g = ac.n
            em.op("sync", "dma_start",
                  out=gah[o * 128:(o + 1) * 128, t0:t0 + MT],
                  in_=ga_s[:, gs, :], waits=[(ac, a_g)], inc=stg[gs], incn=16)
            ga_free[gs] = stg[gs].n
        a_evac = ac.n
        em.wait("tensor", ac, a_evac)
        em.phase = "scan"
        lmark = attn1.lmark

        def build_dd(h):
            hs = h % 2
            wts = [(dv, setup.marks["dv"]), (ac, setup.marks["ac"]),
                   (dv, dv.n)]
            em.op("scalar", "activation",
                  ddT[:, hs].rearrange("p a b -> p (a b)"),
                  diffc[:].rearrange("p a b -> p (a b)"), AF.Exp,
                  scale=snegrep[:, h:h + 1], waits=wts, inc=ac)
            v_dd = ac.n
            em.op("scalar", "activation", qdec[:, hs, :], mbase[:], AF.Exp,
                  scale=snegrep[:, h:h + 1], inc=ac)
            return v_dd, ac.n

        def stage1(h, b, v_dd, a_qd):
            hs = h % 2
            blkg = mt * NB + b
            mcols = slice(b * BLK, (b + 1) * BLK)
            sc = scan["o_ctr"]
            scan["o_ctr"] += 1
            if blkg > 0 and b == 0:
                wts = []
                if scan["Lb_free"][hs]:
                    wts.append((pe, scan["Lb_free"][hs]))
                em.op("vector", "tensor_copy", Lbf[:, hs, :], L_sb[:, h, :],
                      waits=wts, inc=dv)
                scan["Lb_mark"][hs] = dv.n
            wts = [(ac, a_qd), (ac, a_evac)]
            if scan["qs_free"][hs]:
                wts.append((pe, scan["qs_free"][hs]))
            em.op("vector", "tensor_mul", qs_s[:, hs, :], qv(h)[:, mcols],
                  qdec[:, hs, :], waits=wts, inc=dv)
            v_qs = dv.n
            em.wait("vector", dv, v_qs)
            qgs = h % 2
            wts = [(stq[qgs], qg_st_free[qgs])] if (qg_st_free[qgs] and b == 0) else []
            em.op("vector", "tensor_scalar_mul",
                  qg_s[:, qgs, b * BLK:(b + 1) * BLK], qs_s[:, hs, :],
                  Db[:, blkg, h:h + 1], waits=wts, inc=dv)
            v_qg = dv.n
            if b == NB - 1:
                em.op("sync", "dma_start",
                      out=qgh[h * 128:(h + 1) * 128, t0:t0 + MT],
                      in_=qg_s[:, qgs, :], waits=[(dv, v_qg)],
                      inc=stq[qgs], incn=16)
                qg_st_free[qgs] = stq[qgs].n
            qkpi = sc % 2
            wts = psfree(qkpi)
            for ncx in range(2):
                em.op("tensor", "matmul",
                      PS[qkpi][:, ncx * 256:(ncx + 1) * 256],
                      kv_(h)[:, b * BLK + ncx * 128:b * BLK + (ncx + 1) * 128],
                      qv(h)[:, mcols], start=True, stop=True,
                      waits=wts if ncx == 0 else (), inc=pe)
            p_qk = pe.n
            trp = sc % 2
            trdst = PTR if trp == 0 else PS6B
            wts = [(gp, setup.marks["gp"])]
            if scan["tr_free"][trp]:
                wts.append((ac, scan["tr_free"][trp]))
            if trp == 1 and psum_free[6]:
                wts.append((psum_free_sem[6], psum_free[6]))
                psum_free[6] = 0
            for ncx in range(2):
                em.op("tensor", "transpose",
                      trdst[:, ncx * 128:(ncx + 1) * 128],
                      kv_(h)[:, b * BLK + ncx * 128:b * BLK + (ncx + 1) * 128],
                      ident[:], waits=wts if ncx == 0 else (), inc=pe)
            p_tr = pe.n
            cpi = 2 + sc % 2
            n_o = 0
            if blkg > 0:
                wts = [(dv, scan["Lb_mark"][hs]), (dv, v_qs)] + psfree(cpi)
                em.op("tensor", "matmul", PS[cpi][:, 256:512],
                      Lbf[:, hs, :], qs_s[:, hs, :], start=True, stop=False,
                      waits=wts, inc=pe)
                n_o = 1
                scan["Lb_free"][hs] = pe.n
            scan["qs_free"][hs] = pe.n
            qkms = sc % 2
            wts = [(pe, p_qk), (ac, v_dd)]
            if scan["qkm_free"][qkms]:
                wts.append((pe, scan["qkm_free"][qkms]))
            em.op("vector", "tensor_tensor",
                  qkm_s[:, qkms, :, :].rearrange("p a b -> p (a b)"),
                  PS[qkpi][:],
                  ddT[:, hs].rearrange("p a b -> p (a b)"),
                  ALU.mult, waits=wts, inc=dv)
            v_qkm = dv.n
            set_psfree(qkpi, dv, v_qkm)
            wts = [(pe, p_tr)]
            if scan["kT_free"][trp]:
                wts.append((pe, scan["kT_free"][trp]))
            for ncx in range(2):
                em.op("scalar", "activation", kT_s[:, trp, ncx, :],
                      trdst[:, ncx * 128:(ncx + 1) * 128],
                      AF.Identity, scale=kdecT[:, h, ncx:ncx + 1],
                      waits=wts if ncx == 0 else (), inc=ac)
            a_kT = ac.n
            scan["tr_free"][trp] = a_kT
            return dict(sc=sc, v_qkm=v_qkm, a_kT=a_kT, n_o=n_o, h=h, b=b)

        def stage2(st):
            h, b = st["h"], st["b"]
            hs = h % 2
            sc = st["sc"]
            qkms = sc % 2
            trp = sc % 2
            cpi = 2 + sc % 2
            trdst = PTR if trp == 0 else PS6B
            oreg = PS[cpi][:, 256:512]
            for ncx in range(2):
                tc = b * 2 + ncx
                wts = []
                if ncx == 0:
                    wts = [(dv, st["v_qkm"])]
                    if st["n_o"] == 0:
                        wts += psfree(cpi)
                em.op("tensor", "matmul", oreg,
                      vT(tc)[:, h * 128:(h + 1) * 128],
                      qkm_s[:, qkms, ncx, :],
                      start=(st["n_o"] == 0 and ncx == 0), stop=(ncx == 1),
                      waits=wts, inc=pe)
            p_o = pe.n
            scan["qkm_free"][qkms] = p_o
            wts = [(ac, st["a_kT"])]
            for ncx in range(2):
                tc = b * 2 + ncx
                em.op("tensor", "matmul", PS[cpi][:, 0:128],
                      kT_s[:, trp, ncx, :],
                      vT(tc)[:, h * 128:(h + 1) * 128],
                      start=(ncx == 0), stop=(ncx == 1),
                      waits=wts if ncx == 0 else (), inc=pe)
            p_c = pe.n
            scan["kT_free"][trp] = p_c
            em.op("vector", "scalar_tensor_tensor", L_sb[:, h, :],
                  L_sb[:, h, :], Db[:, 1, h:h + 1], PS[cpi][:, 0:128],
                  ALU.mult, ALU.add, waits=[(pe, p_c)], inc=dv)
            set_psfree(cpi, dv, dv.n)
            lmark[h] = dv.n
            em.wait("vector", dv, dv.n)
            if b + 1 < NB:
                wts = []
                if scan["Lb_free"][hs]:
                    wts.append((pe, scan["Lb_free"][hs]))
                em.op("vector", "tensor_copy", Lbf[:, hs, :], L_sb[:, h, :],
                      waits=wts, inc=dv)
                scan["Lb_mark"][hs] = dv.n
            osl = h % 2
            wts = [(pe, p_o)]
            if scan["oS_free"][osl] and b == 0:
                wts.append((sto[osl], scan["oS_free"][osl]))
            em.op("vector", "tensor_copy",
                  o_s[:, osl, b * BLK:(b + 1) * BLK], oreg,
                  waits=wts, inc=dv)
            set_psfree(cpi, dv, dv.n)
            v_oe = dv.n
            if b == NB - 1:
                em.op("sync", "dma_start",
                      out=oh[h * 128:(h + 1) * 128, t0:t0 + MT],
                      in_=o_s[:, osl, :], waits=[(dv, v_oe)],
                      inc=sto[osl], incn=16)
                scan["oS_free"][osl] = sto[osl].n

        pf_mark = [0]

        def ln1_fill_all():
            t1n = (mt + 1) * MT
            for k in range(DC):
                w = []
                if k == 0:
                    w = [(pe, xbg_rd["pe"]), (dv, xbg_rd["dv"])]
                em.op("gpsimd", "dma_start", out=XBG[:, k, :],
                      in_=xin[k * 128:(k + 1) * 128, t1n:t1n + MT],
                      waits=w, inc=sx[0], incn=16)
            pf_mark[0] = sx[0].n

        def ln1_stats(k):
            sl = k % 2
            wq = [(sx[0], pf_mark[0])]
            if sq_war[sl]:
                wq.append((pe, sq_war[sl]))
            em.op("scalar", "activation", SQ[:, sl, :], XBG[:, k, :],
                  AF.Square, waits=wq, inc=ac)
            va = ac.n
            wts = [(sx[0], pf_mark[0])] + (psfree(4) if k == 0 else [])
            em.op("tensor", "matmul", PS[4][:], ones_bf[:], XBG[:, k, :],
                  start=(k == 0), stop=(k == DC - 1), waits=wts, inc=pe)
            wts = [(ac, va)] + (psfree(5) if k == 0 else [])
            em.op("tensor", "matmul", PS[5][:], ones_bf[:], SQ[:, sl, :],
                  start=(k == 0), stop=(k == DC - 1), waits=wts, inc=pe)
            sq_war[sl] = pe.n

        if mt < NMT - 1:
            ln1_fill_all()
        for g in range(NH // 2):
            hA, hB = 2 * g, 2 * g + 1
            ddA = build_dd(hA)
            ddB = build_dd(hB)
            for b in range(NB):
                sA = stage1(hA, b, *ddA)
                sB = stage1(hB, b, *ddB)
                stage2(sA)
                stage2(sB)
            if mt < NMT - 1 and g >= 4:
                ln1_stats(2 * (g - 4))
                ln1_stats(2 * (g - 4) + 1)
        if mt < NMT - 1:
            for k in range(8, DC):
                ln1_stats(k)
        scan["dd_free"][0] = dv.n
        scan["dd_free"][1] = dv.n
        if scan["tr_free"][1]:
            set_psfree(6, ac, scan["tr_free"][1])
        if mt < NMT - 1:
            stats_pre[0] = pe.n
        freeH.append((pe, pe.n))
        freeH.append((dv, dv.n))
        freeL.append((pe, pe.n))
        freeX.append((pe, pe.n))
        set_psfree(6, dv, max(scan["o_free"]))
        if mt == 0:
            wbmarks["attn"] = swb.n
        if mt == 1:
            bg_emit(10 ** 9)

    def cc_phase():
        v_all = dv.n
        p_all = pe.n
        bg_emit(10 ** 9)
        em.op("gpsimd", "dma_start",
              out=cci.rearrange("(h d) e -> d h e", d=128),
              in_=L_sb[:],
              waits=[(dv, v_all)], inc=scc, incn=16)
        em.wait("gpsimd", scc, scc.n)
        em.op("gpsimd", "collective_compute", "AllGather", ALU.bypass,
              replica_groups=[[0, 1, 2, 3], [4, 5, 6, 7]],
              ins=[cci], outs=[cco], inc=ccd)
        wH = list(freeH)
        freeH.clear()
        pf = {}
        em.wait("sync", sto[0], sto[0].n)
        em.wait("sync", sto[1], sto[1].n)
        em.wait("sync", stq[0], stq[0].n)
        em.wait("sync", stq[1], stq[1].n)
        em.wait("sync", stg[0], stg[0].n)
        em.wait("sync", stg[1], stg[1].n)
        em.op("sync", "dma_start",
              out=H_f32[:, 0:8192].rearrange("p (h t) -> p h t", h=NH),
              in_=oh[:, 0:MT].rearrange("(h p) t -> p h t", p=128),
              waits=wH, inc=sol, incn=16)
        pf["o_ld"] = sol.n
        em.op("sync", "dma_start",
              out=AR_L[:].rearrange("p (h t) -> p h t", h=NH),
              in_=gah[:, 0:MT].rearrange("(h p) t -> p h t", p=128),
              waits=list(freeL), inc=sgl, incn=16)
        freeL.clear()
        pf["g_ld"] = sgl.n
        for hh in range(2):
            em.op("sync", "dma_start", out=qgl_s[:, hh, :],
                  in_=qgh[hh * 128:(hh + 1) * 128, 0:MT],
                  inc=sqg[hh], incn=16)
        pf["qg_ld"] = [sqg[0].n, sqg[1].n]
        em.wait("gpsimd", ccd, 1)
        Tj = [H_f32[:, 8192:10240].rearrange("p (h e) -> p h e", h=NH),
              H_f32[:, 8192:10240].rearrange("p (h e) -> p h e", h=NH)]
        Sst = H_f32[:, 10240:12288]
        tmp = H_f32[:, 12288:14336]
        em.op("vector", "memset", Sst, 0.0, waits=wH, inc=dv)
        v_z = dv.n
        mult_mark = v_z
        for j in range(4):
            ssem = scc if j % 2 == 0 else scc2
            wts = [(pe, p_all), (dv, mult_mark)]
            em.op("gpsimd", "dma_start", out=Tj[0][:, :, :],
                  in_=cco[j * 2048:(j + 1) * 2048].rearrange(
                      "(h d) e -> d h e", d=128),
                  waits=wts, inc=ssem, incn=16)
            ld_mark = (ssem, ssem.n)
            wbc = wrep[:, j, :].unsqueeze(2).broadcast_to([128, NH, HD])
            em.op("vector", "tensor_tensor",
                  tmp.rearrange("p (h e) -> p h e", h=NH),
                  Tj[0][:, :, :], wbc, ALU.mult,
                  waits=[ld_mark, (dv, setup.marks["dv"]),
                         setup.marks["wrep"]], inc=dv)
            mult_mark = dv.n
            em.op("vector", "tensor_tensor", Sst, Sst, tmp, ALU.add, inc=dv)
        last = dv.n
        em.op("vector", "tensor_copy", Ssbf.rearrange("p h e -> p (h e)"),
              Sst, waits=[(dv, last)], inc=dv)
        freeH.append((dv, dv.n))
        return dv.n, pf

    qg_free = [0, 0]
    var_mm_last = [0]

    def pass2(mt, ss_mark, pf=None):
        t0 = mt * MT
        if pf is None:
            em.wait("sync", sto[0], sto[0].n)
            em.wait("sync", sto[1], sto[1].n)
            em.wait("sync", stq[0], stq[0].n)
            em.wait("sync", stq[1], stq[1].n)
            em.wait("sync", stg[0], stg[0].n)
            em.wait("sync", stg[1], stg[1].n)
            o_lds = []
            g_lds = []
            wtsH = list(freeH)
            freeH.clear()
            wtsL = list(freeL)
            freeL.clear()
            for h in range(NH):
                em.op("sync", "dma_start",
                      out=H_f32[:, h * MT:(h + 1) * MT],
                      in_=oh[h * 128:(h + 1) * 128, t0:t0 + MT],
                      waits=(wtsH if h == 0 else []), inc=sol, incn=16)
                o_lds.append(sol.n)
                em.op("sync", "dma_start",
                      out=AR_L[:, h * MT:(h + 1) * MT],
                      in_=gah[h * 128:(h + 1) * 128, t0:t0 + MT],
                      waits=(wtsL if h == 0 else []), inc=sgl, incn=16)
                g_lds.append(sgl.n)
            qg_ld = [0, 0]
        else:
            o_lds = [pf["o_ld"]] * NH
            g_lds = [pf["g_ld"]] * NH
            qg_ld = list(pf["qg_ld"])
            freeH.clear()
            freeL.clear()

        def qg_issue(h):
            s = h % 2
            w = [(pe, qg_free[s])] if qg_free[s] else []
            em.op("sync", "dma_start", out=qgl_s[:, s, :],
                  in_=qgh[h * 128:(h + 1) * 128, t0:t0 + MT],
                  waits=w, inc=sqg[s], incn=16)
            qg_ld[s] = sqg[s].n

        if pf is None:
            qg_issue(0)
            qg_issue(1)
        em.phase = "pass2-oc"
        for h in range(NH):
            s = h % 2
            pi = s
            wts = [(sqg[s], qg_ld[s]), (dv, ss_mark)] + psfree(pi)
            em.op("tensor", "matmul", PS[pi][:], Ssbf[:, h, :], qgl_s[:, s, :],
                  start=True, stop=True, waits=wts, inc=pe)
            p_oc = pe.n
            qg_free[s] = p_oc
            if h + 2 < NH:
                qg_issue(h + 2)
            em.op("vector", "tensor_tensor", ofull(h), ofull(h), PS[pi][:],
                  ALU.add, waits=[(pe, p_oc), (sol, o_lds[h])], inc=dv)
            set_psfree(pi, dv, dv.n)
            em.wait("vector", dv, dv.n)
            v_of = dv.n
            osl2 = h % 2
            wsq = [(dv, v_of), (pe, var_mm_last[0])]
            if sq_war[osl2]:
                wsq.append((pe, sq_war[osl2]))
            em.op("scalar", "activation", SQ[:, osl2, :], ofull(h), AF.Square,
                  waits=wsq, inc=ac)
            a_sq = ac.n
            wts = [(ac, a_sq)] + (psfree(4) if h == 0 else [])
            em.op("tensor", "matmul", PS[4][:], ones_bf[:], SQ[:, osl2, :],
                  start=(h == 0), stop=(h == NH - 1), waits=wts, inc=pe)
            var_mm_last[0] = pe.n
            sq_war[osl2] = pe.n
        p_var = pe.n
        em.op("vector", "tensor_scalar", R2[:], PS[4][:], 1.0 / D, EPS_RMS,
              ALU.mult, ALU.add, waits=[(pe, p_var)], inc=dv)
        set_psfree(4, dv, dv.n)
        em.wait("vector", dv, dv.n)
        em.op("vector", "reciprocal", R2[:], R2[:], inc=dv)
        v_r2 = dv.n
        em.op("scalar", "activation", R2[:], R2[:], AF.Sqrt,
              waits=[(dv, v_r2)], inc=ac)
        a_R = ac.n
        z_marks = []
        for h in range(NH):
            em.op("vector", "scalar_tensor_tensor", t1_s[:], ofull(h),
                  nw_sb[:, h:h + 1], R2[:], ALU.mult, ALU.mult,
                  waits=[(ac, a_R)], inc=dv)
            em.wait("vector", dv, dv.n)
            em.op("vector", "tensor_tensor", gal(h), t1_s[:], gal(h),
                  ALU.mult, waits=[(sgl, g_lds[h])], inc=dv)
            z_marks.append(dv.n)
        v_z = dv.n
        freeH.append((dv, v_z))
        stW = WStream(swA, sfA, wgs, (pe, pe.n), nslots=3)

        def osrc(f):
            return wob[f]

        em.phase = "outproj"
        stW.issue(0, osrc(0), True)
        stW.issue(1, osrc(1), True)
        stW.issue(2, osrc(2), True)
        for f in range(DC):
            s = f % 3
            pi = f % 4
            wts = [stW.loaded[s]] + psfree(pi)
            for k in range(DC):
                kw = list(wts) if k == 0 else []
                if f == 0:
                    kw.append((dv, z_marks[k]))
                em.op("tensor", "matmul", PS[pi][:], wgs(s)[:, k, :], gal(k),
                      start=(k == 0), stop=(k == DC - 1),
                      waits=kw,
                      inc=(pe if k == DC - 1 else None))
            stW.consumed(f, (pe, pe.n))
            p_mm = pe.n
            if f + 3 < DC:
                stW.issue(f + 3, osrc(f + 3), True)
            sl = f % 2
            wxd = [(std[sl], std[sl].n)]
            if xd_free[sl][0]:
                wxd += [(dv, xd_free[sl][0]), (ac, xd_free[sl][1])]
            em.op("sync", "dma_start", out=xd_s[:, sl, :],
                  in_=x1h[f * 128:(f + 1) * 128, t0:t0 + MT],
                  waits=wxd, inc=sxd[sl], incn=16)
            x_ld = sxd[sl].n
            em.op("vector", "tensor_tensor", xd_s[:, sl, :], PS[pi][:],
                  xd_s[:, sl, :], ALU.add,
                  waits=[(pe, p_mm), (sxd[sl], x_ld)], inc=dv)
            set_psfree(pi, dv, dv.n)
            v_x2 = dv.n
            em.op("sync", "dma_start",
                  out=x2h[f * 128:(f + 1) * 128, t0:t0 + MT],
                  in_=xd_s[:, sl, :], waits=[(dv, v_x2)], inc=std[sl], incn=16)
            wpc = [(pe, sq_war[sl])] if sq_war[sl] else []
            if f == 0:
                wpc.append((pe, xbg_rd["pe"]))
            em.op("vector", "tensor_copy", XBG[:, f, :],
                  xd_s[:, sl, :], waits=wpc, inc=dv)
            vb = dv.n
            em.op("scalar", "activation", SQ[:, sl, :],
                  xd_s[:, sl, :], AF.Square, waits=[(dv, v_x2)] + wpc, inc=ac)
            va = ac.n
            wts = [(dv, vb)] + (psfree(4) if f == 0 else [])
            em.op("tensor", "matmul", PS[4][:], ones_bf[:],
                  XBG[:, f, :], start=(f == 0), stop=(f == DC - 1),
                  waits=wts, inc=pe)
            wts = [(ac, va)] + (psfree(5) if f == 0 else [])
            em.op("tensor", "matmul", PS[5][:], ones_bf[:],
                  SQ[:, sl, :], start=(f == 0), stop=(f == DC - 1),
                  waits=wts, inc=pe)
            sq_war[sl] = pe.n
            xd_free[sl] = (vb, va)
        stats_pre[0] = pe.n
        freeL.append((pe, pe.n))

    for pair in ((wob, wo, DC), (wgb, wg, HC), (wub, wu, HC)):
        dst_t, src_t, n = pair
        for i in range(n):
            bg_jobs.append((dst_t[i], src_t[i]))
    for i in range(DC):
        bg_jobs.append((wdb[i], wd[i]))

    setup()
    for mt in range(NMT):
        pre = ffn(mt, 0, x1h)
        attn1(mt, pre)
    ss_mark, pf0 = cc_phase()
    for mt in range(NMT):
        pass2(mt, ss_mark, pf0 if mt == 0 else None)
        ffn(mt, 1, out)
    em.wait("sync", std[0], std[0].n)
    em.wait("sync", std[1], std[1].n)

    with nc.Block() as block:
        @block.sync
        def _(e):
            em.replay("sync", e)

        @block.gpsimd
        def _(e):
            em.replay("gpsimd", e)

        @block.vector
        def _(e):
            em.replay("vector", e)

        @block.scalar
        def _(e):
            em.replay("scalar", e)

        @block.tensor
        def _(e):
            em.replay("tensor", e)

    es.close()
    nc._em_labels = em.mm_labels
    return nc


_NC_CACHE = []


def _stage_lhsT(w):
    w = np.asarray(w)
    O, K = w.shape
    wT = np.ascontiguousarray(w.T)
    v = wT.reshape(K // 128, 128, O // 128, 128)
    return np.ascontiguousarray(v.transpose(2, 1, 0, 3))


def kernel(L0_attn_hidden_states, L0_attn_slope_rate, L0_attn_w_qkv,
           L0_attn_w_gate, L0_attn_w_out, L0_attn_norm_w,
           L0_ffn_wq_gate, L0_ffn_s_gate, L0_ffn_wq_up, L0_ffn_s_up,
           L0_ffn_wq_down, L0_ffn_s_down):
    x = np.asarray(L0_attn_hidden_states)
    xs = np.ascontiguousarray(x.reshape(16384, D))

    wg_s = _stage_lhsT(L0_ffn_wq_gate)
    wu_s = _stage_lhsT(L0_ffn_wq_up)
    wdT = np.ascontiguousarray(np.asarray(L0_ffn_wq_down).T)
    v = wdT.reshape(2, 22, 128, DC, 128)
    wd_s = np.ascontiguousarray(v.transpose(3, 0, 2, 1, 4))
    wqkv_s = _stage_lhsT(L0_attn_w_qkv)
    _vb = wqkv_s[2::3]
    wv_s = np.ascontiguousarray(np.concatenate([_vb[0::2], _vb[1::2]],
                                               axis=-1))
    wga_s = _stage_lhsT(L0_attn_w_gate)
    wo_s = _stage_lhsT(L0_attn_w_out)
    sg_s = np.ascontiguousarray(
        np.asarray(L0_ffn_s_gate).reshape(HC, 128).T.astype(np.float32))
    su_s = np.ascontiguousarray(
        np.asarray(L0_ffn_s_up).reshape(HC, 128).T.astype(np.float32))
    sd_s = np.ascontiguousarray(
        np.asarray(L0_ffn_s_down).reshape(DC, 128).T.astype(np.float32))
    nw_s = np.ascontiguousarray(
        np.asarray(L0_attn_norm_w).reshape(DC, 128).T.astype(np.float32))
    sl_s = np.ascontiguousarray(
        np.asarray(L0_attn_slope_rate).reshape(1, NH).astype(np.float32))

    if not _NC_CACHE:
        _NC_CACHE.append(build_program())
    nc = _NC_CACHE[0]
    in_maps = []
    for c in range(NCORES):
        g = c % 4
        ccm_c = np.zeros((1, 4), np.float32)
        ccp_c = np.zeros((1, 4), np.float32)
        for j in range(4):
            if j < g:
                ccm_c[0, j] = 1.0
                ccp_c[0, j] = -2048.0 * (g - 1 - j)
        xT = np.ascontiguousarray(xs[c * T:(c + 1) * T].T)
        in_maps.append({
            "xin": xT, "wg": wg_s, "wu": wu_s, "wd": wd_s,
            "wqkv": wqkv_s, "wvin": wv_s, "wga": wga_s, "wo": wo_s,
            "sgate": sg_s, "sup": su_s, "sdown": sd_s, "nwin": nw_s,
            "slopes": sl_s, "ccm": ccm_c, "ccp": ccp_c,
        })
    res = run_bass_kernel_spmd(nc, in_maps, core_ids=list(range(NCORES)))
    outs = [np.asarray(res.results[c]["out"]).T for c in range(NCORES)]
    full = np.concatenate(outs, axis=0).reshape(2, 8192, D)
    return full.astype(np.float32)
